# version 03 (final)
# baseline (speedup 1.0000x reference)
import numpy as np
from contextlib import ExitStack

import concourse.bass as bass
import concourse.mybir as mybir
from concourse.bass_utils import run_bass_kernel_spmd

F32 = mybir.dt.float32
BF16 = mybir.dt.bfloat16
ALU = mybir.AluOpType
ACT = mybir.ActivationFunctionType
AXX = mybir.AxisListType.X

NCORE = 8
B, S, D = 2, 16384, 1024
TPC = S // 4
TT = 512
MAGIC = 12582912.0
BIG = 1.0e30

N_STAGES = 8
S5_CUT = 99
S5_FIRN = 4
S5_FIRV = 0
S5_RC = 4

_DTS = {F32: 4, BF16: 2, mybir.dt.int32: 4}

ENGS = ("tensor", "vector", "scalar", "gpsimd", "sync")
QUEUE_ENG = {"qS": "sync", "qG": "gpsimd", "qV": "vector", "qC": "scalar"}
ENG_SEM = {"tensor": "eT", "vector": "eV", "scalar": "eS", "gpsimd": "eG"}


def asap(x):
    return x if hasattr(x, "tensor") else x[:, :]


def bbox(ap):
    ap = asap(ap)
    t = ap.tensor
    shape = list(t.shape)
    o = ap.offset
    dims = [list(dd) for dd in ap.ap]
    tname = t.name
    if type(t).__name__ == "DRamTensorHandle":
        lo = hi = o
        for s, c in dims:
            d = s * (c - 1)
            if d >= 0:
                hi += d
            else:
                lo += d
        return (tname, 0, 0, lo, hi)
    F = shape[-1]
    p_lo = p_hi = o // F
    f_lo = f_hi = o % F
    ok = True
    for s, c in dims:
        if c == 1:
            continue
        if s % F == 0:
            d = (s // F) * (c - 1)
            if d >= 0:
                p_hi += d
            else:
                p_lo += d
        else:
            d = s * (c - 1)
            if d >= 0:
                f_hi += d
            else:
                f_lo += d
    if f_lo < 0 or f_hi >= F or p_lo < 0 or p_hi >= shape[0]:
        ok = False
    if not ok:
        return (tname, 0, shape[0] - 1, 0, F - 1)
    return (tname, p_lo, p_hi, f_lo, f_hi)


def _overlap(a, b):
    return not (a[1] < b[0] or b[1] < a[0] or a[3] < b[2] or b[3] < a[2])


class Mem:
    def __init__(self):
        self.w = {}
        self.r = {}

    def deps_read(self, box):
        t = box[0]
        bb = box[1:]
        return [rec[4] for rec in self.w.get(t, ()) if _overlap(bb, rec)]

    def deps_write(self, box):
        t = box[0]
        bb = box[1:]
        deps = [rec[4] for rec in self.w.get(t, ()) if _overlap(bb, rec)]
        deps += [rec[4] for rec in self.r.get(t, ()) if _overlap(bb, rec)]
        return deps

    def _covers(self, bb, rec):
        return (bb[0] <= rec[0] and bb[1] >= rec[1] and bb[2] <= rec[2]
                and bb[3] >= rec[3])

    def add_read(self, box, tok):
        t = box[0]
        self.r.setdefault(t, []).append((box[1], box[2], box[3], box[4], tok))

    def add_write(self, box, tok):
        t = box[0]
        bb = box[1:]
        lst = self.w.setdefault(t, [])
        lst[:] = [rec for rec in lst if not self._covers(bb, rec)]
        lst.append((box[1], box[2], box[3], box[4], tok))
        rl = self.r.get(t)
        if rl:
            rl[:] = [rec for rec in rl if not self._covers(bb, rec)]


SEM_CAP = 57600


class P:
    def __init__(self, tpc=TPC):
        self.tpc = tpc
        self.ntt = tpc // TT
        self.nc = bass.Bass("TRN2", target_bir_lowering=False)
        self.st = ExitStack()
        self.q = {e: [] for e in ENGS}
        self.g = {}
        self.sems = {}
        self.epoch = {}
        self.frozen = {}
        for nm in ("eT", "eV", "eS", "eG", "qS", "qG", "qV", "qC", "cc"):
            self.epoch[nm] = 0
            self.sems[nm + "#0"] = self.st.enter_context(self.nc.semaphore(nm))
        self.cnt = {k: 0 for k in self.sems}
        self.know = {e: {} for e in ENGS}
        self.snaps = {}
        self.qbar_pending = {"qS": 0, "qG": 0, "qV": 0, "qC": 0}
        self.qbar_done = {"qS": 0, "qG": 0, "qV": 0, "qC": 0}
        self._rr = {"qS": 0, "qG": 0}
        self.mem = Mem()
        self._cstpool = {}
        self.nops = 0

    def _key(self, base):
        return base + "#%d" % self.epoch[base]

    def _base(self, key):
        return key.split("#", 1)[0]

    def _switch(self, base, mult):
        key = self._key(base)
        if (self.cnt[key] + 1) * mult <= SEM_CAP:
            return
        if base in self.qbar_pending:
            eng = QUEUE_ENG[base]
            h = self.sems[key]
            v = self.cnt[key] * 16
            self.q[eng].append(lambda E, h=h, v=v: E.wait_ge(h, v))
            self.know[eng][key] = v
            self.nops += 1
            self.qbar_pending[base] = 0
            self.qbar_done[base] = 0
            self.frozen[key] = v
        else:
            self.frozen[key] = self.cnt[key]
        self.epoch[base] += 1
        nk = self._key(base)
        self.sems[nk] = self.st.enter_context(self.nc.semaphore(nk))
        self.cnt[nk] = 0

    def din(self, name, shape, dtype=F32):
        self.g[name] = self.nc.dram_tensor(name, list(shape), dtype,
                                           kind="ExternalInput").ap()
        return self.g[name]

    def dout(self, name, shape, dtype=F32):
        self.g[name] = self.nc.dram_tensor(name, list(shape), dtype,
                                           kind="ExternalOutput").ap()
        return self.g[name]

    def dtmp(self, name, shape, dtype=F32):
        self.g[name] = self.nc.dram_tensor(name, list(shape), dtype).ap()
        return self.g[name]

    def sbuf(self, name, shape, dtype=F32):
        t = self.st.enter_context(self.nc.sbuf_tensor(name, list(shape), dtype))
        self.g[name] = t
        return t

    def psum(self, name, shape, dtype=F32):
        t = self.st.enter_context(self.nc.psum_tensor(name, list(shape), dtype))
        self.g[name] = t
        return t

    def _merge_snap(self, eng, tok):
        sn = self.snaps.get(tok)
        if sn:
            k = self.know[eng]
            for s, v in sn.items():
                if k.get(s, 0) < v:
                    k[s] = v

    def _wait(self, eng, tok):
        sem, val = tok
        k = self.know[eng]
        base = self._base(sem)
        if sem in self.frozen and base in self.qbar_pending:
            val = self.frozen[sem]
        elif base in self.qbar_pending:
            n_now = self.cnt[sem]
            val = n_now * 16
            if self.qbar_pending[base] < n_now:
                self.qbar_pending[base] = n_now
        if k.get(sem, 0) >= val:
            return
        h = self.sems[sem]
        self.q[eng].append(lambda E, h=h, v=val: E.wait_ge(h, v))
        self.nops += 1
        k[sem] = val
        self._merge_snap(eng, (sem, val))

    def _flush_qbar(self, base):
        if self.qbar_pending[base] > self.qbar_done[base]:
            key = self._key(base)
            v = self.qbar_pending[base] * 16
            eng = QUEUE_ENG[base]
            h = self.sems[key]
            self.q[eng].append(lambda E, h=h, v=v: E.wait_ge(h, v))
            self.know[eng][key] = max(self.know[eng].get(key, 0), v)
            self.qbar_done[base] = self.qbar_pending[base]
            self.nops += 1

    def _emit(self, eng, meth, a, kw, base, inc):
        self._switch(base, inc)
        key = self._key(base)
        h = self.sems[key]
        def fn(E, meth=meth, a=a, kw=kw, h=h, inc=inc):
            getattr(E, meth)(*a, **kw).then_inc(h, inc)
        self.q[eng].append(fn)
        self.cnt[key] += 1
        self.nops += 1
        tok = (key, self.cnt[key] * (16 if inc == 16 else 1))
        self.snaps[tok] = dict(self.know[eng])
        return tok

    def op(self, eng, meth, *a, reads=(), writes=(), **kw):
        extra = [v for v in list(a) + list(kw.values())
                 if isinstance(v, bass.AP) and not any(v is w for w in writes)
                 and not any(v is r for r in reads)]
        rb = [bbox(x) for x in list(reads) + extra]
        wb = [bbox(x) for x in writes]
        deps = []
        for b_ in rb:
            deps += self.mem.deps_read(b_)
        for b_ in wb:
            deps += self.mem.deps_write(b_)
        for tok in deps:
            self._wait(eng, tok)
        sem = ENG_SEM[eng]
        tok = self._emit(eng, meth, a, kw, sem, 1)
        for b_ in rb:
            self.mem.add_read(b_, tok)
        for b_ in wb:
            self.mem.add_write(b_, tok)
        return tok

    def dma(self, qsem, out, in_, **kw):
        if asap(out).dtype != asap(in_).dtype:
            qsem = "qG"
        else:
            self._rr["n"] = (self._rr.get("n", 0) + 1) % 3
            qsem = ("qS", "qG", "qC")[self._rr["n"]]
        eng = QUEUE_ENG[qsem]
        rb, wb = bbox(in_), bbox(out)
        deps = self.mem.deps_read(rb) + self.mem.deps_write(wb)
        for tok in deps:
            self._wait(eng, tok)
        self._flush_qbar(qsem)
        kw = dict(kw)
        kw["out"] = out
        kw["in_"] = in_
        tok = self._emit(eng, "dma_start", (), kw, qsem, 16)
        self.mem.add_read(rb, tok)
        self.mem.add_write(wb, tok)
        return tok

    def coll(self, kind, op_, groups, ins, outs):
        eng = "gpsimd"
        rbs = [bbox(x) for x in ins]
        wbs = [bbox(x) for x in outs]
        deps = []
        for b_ in rbs:
            deps += self.mem.deps_read(b_)
        for b_ in wbs:
            deps += self.mem.deps_write(b_)
        for tok in deps:
            self._wait(eng, tok)
        tok = self._emit(eng, "collective_compute",
                         (kind, op_),
                         {"replica_groups": groups, "ins": ins, "outs": outs},
                         "cc", 1)
        for b_ in rbs:
            self.mem.add_read(b_, tok)
        for b_ in wbs:
            self.mem.add_write(b_, tok)
        return tok

    def mm(self, out, terms, cont=False, stop=True):
        wb = bbox(out)
        deps = []
        if not cont:
            deps += self.mem.deps_write(wb)
        rbs = []
        for l, r in terms:
            rbs.append(bbox(l))
            rbs.append(bbox(r))
        for b_ in rbs:
            deps += self.mem.deps_read(b_)
        for tok in deps:
            self._wait("tensor", tok)
        n = len(terms)
        tok = None
        for i, (l, r) in enumerate(terms):
            tok = self._emit("tensor", "matmul", (),
                            {"out": out, "lhsT": l, "rhs": r,
                             "start": (i == 0 and not cont),
                             "stop": (i == n - 1 and stop)}, "eT", 1)
        for b_ in rbs:
            self.mem.add_read(b_, tok)
        self.mem.add_write(wb, tok)
        return tok

    def barrier(self):
        toks = []
        for b in ("eT", "eV", "eS", "eG", "qS", "qG", "qV", "qC", "cc"):
            k = self._key(b)
            if self.cnt[k] > 0:
                toks.append((k, self.cnt[k] * (16 if b.startswith("q") else 1)))
        for e in ENGS:
            for t in toks:
                self._wait(e, t)

    def cst(self, val, rows=128):
        if val not in self._cstpool:
            col = 300 + len(self._cstpool)
            assert col < 384
            self._cstpool[val] = col
            CF = self.g["CF"]
            self.op("vector", "memset", CF[:, col:col + 1], float(val),
                    writes=[CF[:, col:col + 1]])
        col = self._cstpool[val]
        return self.g["CF"][0:rows, col:col + 1]

    def finish(self):
        with self.nc.Block() as block:
            for e in ENGS:
                getattr(block, e)(lambda E, e=e: [f(E) for f in self.q[e]])



def chv(ap2d, tt=None, w=TT):
    v = ap2d.rearrange("(c p) t -> p c t", p=128)
    if tt is None:
        return v
    return v[:, :, tt * w:(tt + 1) * w]


def colsum(p, out_row, chunks, cont=False, stop=True):
    ones_col = p.g["CF"][:, 0:1]
    p.mm(out_row, [(ones_col, c) for c in chunks], cont=cont, stop=stop)


def bcast(p, out_ps, row):
    ones_row = p.g["CF"][0:1, 4:132]
    row = asap(row)
    p.mm(asap(out_ps)[:, 0:row.shape[-1]], [(ones_row, row)])



def emit_prep(p: P):
    g = p.g
    ntt = p.ntt
    CF = g["CF"]
    p.dma("qS", CF[:, 132:260], g["identf"])
    p.op("vector", "memset", CF[:, 0:1], 1.0, writes=[CF[:, 0:1]])
    p.op("vector", "memset", CF[0:1, 4:132], 1.0, writes=[CF[0:1, 4:132]])
    p.op("scalar", "copy", out=g["CH"][:, 0:128], in_=CF[:, 132:260],
         reads=[CF[:, 132:260]], writes=[g["CH"][:, 0:128]])
    p.op("vector", "memset", g["CH"][:, 128:129], 1.0,
         writes=[g["CH"][:, 128:129]])
    smalls = [("l0a_bg", 8, 0), ("l0a_bv", 8, 8), ("l0a_bd", 8, 16),
              ("l0a_wmix", 8, 24), ("l0a_wffn", 8, 32)]
    for nm, w, c0 in smalls:
        p.dma("qS", g["SM"][:, c0:c0 + w],
              g[nm].rearrange("(c p) -> p c", p=128),
              allow_slow_non_contiguous=True)
    p.dma("qS", g["SM"][:, 40:44], g["cmask"])
    p.dma("qS", g["SM"][:, 44:48], g["cmaskc"])


def _ln_rows(p: P, xv, nch, eps, tmp_sq, ps_a, ps_b, rms_eps=None):
    g = p.g
    W = xv.shape[2]
    for ch in range(nch):
        p.op("scalar", "activation", out=tmp_sq[:, ch, :], in_=xv[:, ch, :],
             func=ACT.Square, reads=[xv[:, ch, :]], writes=[tmp_sq[:, ch, :]])
    colsum(p, ps_a[0:1, 0:W], [xv[:, ch, :] for ch in range(nch)])
    colsum(p, ps_b[0:1, 0:W], [tmp_sq[:, ch, :] for ch in range(nch)])
    ST = g["ST"]
    s0, s1, s2, s3 = (ST[0:1, i * TT:i * TT + W] for i in range(4))
    p.op("vector", "tensor_scalar", out=s0, in0=ps_a[0:1, 0:W],
         scalar1=1.0 / D, scalar2=None, op0=ALU.mult,
         reads=[ps_a[0:1, 0:W]], writes=[s0])
    p.op("vector", "tensor_scalar", out=s1, in0=ps_b[0:1, 0:W],
         scalar1=1.0 / D, scalar2=None, op0=ALU.mult,
         reads=[ps_b[0:1, 0:W]], writes=[s1])
    p.op("vector", "tensor_tensor", out=s3, in0=s0, in1=s0, op=ALU.mult,
         reads=[s0], writes=[s3])
    p.op("vector", "tensor_sub", out=s1, in0=s1, in1=s3,
         reads=[s1, s3], writes=[s1])
    p.op("scalar", "activation", out=s2, in_=s1, func=ACT.Sqrt,
         bias=p.cst(eps, 1), reads=[s1], writes=[s2])
    p.op("vector", "reciprocal", out=s2, in_=s2, reads=[s2], writes=[s2])
    p.op("vector", "scalar_tensor_tensor", out=s0, in0=s0, scalar=-1.0,
         in1=s2, op0=ALU.mult, op1=ALU.mult,
         reads=[s0, s2], writes=[s0])
    rms = None
    if rms_eps is not None:
        p.op("vector", "tensor_tensor", out=s3, in0=s1, in1=s2, op=ALU.mult,
             reads=[s1, s2], writes=[s3])
        p.op("vector", "tensor_tensor", out=s3, in0=s3, in1=s2, op=ALU.mult,
             reads=[s3, s2], writes=[s3])
        p.op("scalar", "activation", out=s3, in_=s3, func=ACT.Sqrt,
             bias=p.cst(rms_eps, 1), reads=[s3], writes=[s3])
        p.op("vector", "reciprocal", out=s3, in_=s3, reads=[s3], writes=[s3])
        rms = s3
    return s2, s0, rms


def emit_ln(p: P, src, dsts, rms_eps=None, rms_dst=None, gamma_c0=None,
            beta_c0=None, sb0=0):
    g = p.g
    SA, SH, ST = g["SA"], g["SH"], g["ST"]
    for tt in range(p.ntt):
        o = sb0 + (tt % 2) * 8192
        xv = SA[:, o:o + 4096].rearrange("p (c t) -> p c t", c=8)
        sq = SA[:, o + 4096:o + 8192].rearrange("p (c t) -> p c t", c=8)
        p.dma("qS", xv, chv(src, tt))
        rstd, nmrs, rms = _ln_rows(p, xv, 8, 1e-5, sq, g["ps0"], g["ps1"],
                                   rms_eps=rms_eps)
        if rms is not None:
            p.dma("qG", rms_dst[tt:tt + 1, :], rms)
        bcast(p, g["ps2"], rstd)
        bcast(p, g["ps3"], nmrs)
        ln = sq
        for ch in range(8):
            p.op("vector", "scalar_tensor_tensor", out=ln[:, ch, :],
                 in0=xv[:, ch, :], scalar=1.0, in1=g["ps2"][:, 0:TT],
                 op0=ALU.mult, op1=ALU.mult,
                 reads=[xv[:, ch, :], g["ps2"][:, 0:TT]], writes=[ln[:, ch, :]])
            p.op("vector", "tensor_add", out=ln[:, ch, :], in0=ln[:, ch, :],
                 in1=g["ps3"][:, 0:TT],
                 reads=[ln[:, ch, :], g["ps3"][:, 0:TT]], writes=[ln[:, ch, :]])
            if gamma_c0 is not None:
                sc = g["SM"][:, gamma_c0 + ch:gamma_c0 + ch + 1]
                if beta_c0 is not None:
                    bt = g["SM"][:, beta_c0 + ch:beta_c0 + ch + 1]
                    p.op("vector", "tensor_scalar", out=ln[:, ch, :],
                         in0=ln[:, ch, :], scalar1=sc, scalar2=bt,
                         op0=ALU.mult, op1=ALU.add,
                         reads=[ln[:, ch, :], sc, bt], writes=[ln[:, ch, :]])
                else:
                    p.op("vector", "tensor_scalar", out=ln[:, ch, :],
                         in0=ln[:, ch, :], scalar1=sc, scalar2=None,
                         op0=ALU.mult, reads=[ln[:, ch, :], sc],
                         writes=[ln[:, ch, :]])
        hb = None
        for dst, dt in dsts:
            if dt == F32:
                p.dma("qG", chv(dst, tt), ln)
            else:
                if hb is None:
                    hb = SH[:, (tt % 2) * 4096:(tt % 2) * 4096 + 4096] \
                        .rearrange("p (c t) -> p c t", c=8)
                    for ch in range(8):
                        p.op("scalar", "copy", out=hb[:, ch, :],
                             in_=ln[:, ch, :],
                             reads=[ln[:, ch, :]], writes=[hb[:, ch, :]])
                tgt = dst(tt) if callable(dst) else chv(dst, tt)
                p.dma("qG", tgt, hb)


def load_w_lhsT(p: P, dram_h, sb_cols, kc_n, m_n, name=None):
    SH = p.g["SH"]
    v = SH[:, sb_cols:sb_cols + kc_n * m_n].rearrange(
        "p (kc m) -> p kc m", kc=kc_n)
    s = dram_h.rearrange("(kc p) m -> p kc m", p=128)
    for k0 in range(0, kc_n, 8):
        k1 = min(kc_n, k0 + 8)
        p.dma("qS", v[:, k0:k1, :], s[:, k0:k1, :])
    return v


def emit_cast_weights(p: P, n_stages):
    g = p.g
    N_STAGES = n_stages
    SA, SH = g["SA"], g["SH"]
    jobs = [("l0a_Wg", "h_Wg", 1024, 1024), ("l0a_Wv", "h_Wv", 1024, 1024),
            ("l0a_Wd", "h_Wd", 1024, 1024),
            ("l0a_Wgate", "h_Wgate", 1024, 3072),
            ("l0a_Wup", "h_Wup", 1024, 3072),
            ("l0a_Wout", "h_Wout", 3072, 1024)]
    if N_STAGES >= 2:
        jobs += [("moe_w1T", "h_w1T", 16 * 1024, 768),
                 ("moe_w3T", "h_w3T", 16 * 1024, 768),
                 ("moe_w2T", "h_w2T", 16 * 768, 1024),
                 ("moe_ws1T", "h_ws1T", 1024, 768),
                 ("moe_ws3T", "h_ws3T", 1024, 768),
                 ("moe_ws2T", "h_ws2T", 768, 1024)]
    if N_STAGES >= 3:
        jobs += [("l0b_W", "h_l0bW", 1024, 3072)]
    if N_STAGES >= 5:
        jobs += [("l1a_Wq", "h_Wq", 1024, 1024), ("l1a_Wk", "h_Wk", 1024, 1024),
                 ("l1a_Wv", "h_Wv1", 1024, 1024),
                 ("l1a_rW1", "h_rW1", 6144, 2048),
                 ("l1a_rW2", "h_rW2", 2048, 128),
                 ("l1a_Wo", "h_Wo", 1024, 1024)]
    for src, dst, rows, cols in jobs:
        rt = rows // 128
        for r in range(rt):
            par = r % 2
            fb = SA[:, par * 8192:par * 8192 + cols]
            hb = SH[:, par * 4096:par * 4096 + cols]
            p.dma("qS", fb, g[src][r * 128:(r + 1) * 128, :])
            half = cols // 2
            p.op("scalar", "copy", out=hb[:, 0:half], in_=fb[:, 0:half],
                 reads=[fb[:, 0:half]], writes=[hb[:, 0:half]])
            p.op("vector", "tensor_copy", out=hb[:, half:cols],
                 in_=fb[:, half:cols],
                 reads=[fb[:, half:cols]], writes=[hb[:, half:cols]])
            p.dma("qG", g[dst][r * 128:(r + 1) * 128, :], hb)


def emit_s1(p: P):
    g = p.g
    ntt = p.ntt
    SA, SH, ST, SM = g["SA"], g["SH"], g["ST"], g["SM"]

    wg = load_w_lhsT(p, g["h_Wg"], 8192, 8, 1024)
    wv = load_w_lhsT(p, g["h_Wv"], 16384, 8, 1024)
    wd = load_w_lhsT(p, g["h_Wd"], 24576, 8, 1024)
    for tt in range(ntt):
        o = (tt % 2) * 8192
        lnv = SA[:, o:o + 4096].rearrange("p (c t) -> p c t", c=8)
        p.dma("qS", lnv, chv(g["ln0f"], tt))
        rmsrow = ST[0:1, 4 * TT:5 * TT]
        p.dma("qS", rmsrow, g["s1_rms"][tt:tt + 1, :])
        bcast(p, g["ps4"], rmsrow)
        xn = SH[:, (tt % 2) * 4096:(tt % 2) * 4096 + 4096].rearrange(
            "p (c t) -> p c t", c=8)
        xnf = SA[:, o + 4096:o + 8192].rearrange("p (c t) -> p c t", c=8)
        for ch in range(8):
            wmix = SM[:, 24 + ch:25 + ch]
            p.op("vector", "scalar_tensor_tensor", out=xnf[:, ch, :],
                 in0=lnv[:, ch, :], scalar=wmix, in1=g["ps4"][:, 0:TT],
                 op0=ALU.mult, op1=ALU.mult,
                 reads=[lnv[:, ch, :], wmix, g["ps4"][:, 0:TT]],
                 writes=[xnf[:, ch, :]])
            p.op("scalar", "copy", out=xn[:, ch, :], in_=xnf[:, ch, :],
                 reads=[xnf[:, ch, :]], writes=[xn[:, ch, :]])
        xscan = SA[:, 16384:20480].rearrange("p (c t) -> p c t", c=8)
        av = SA[:, 20480:24576].rearrange("p (c t) -> p c t", c=8)
        for oc in range(8):
            for wi, wt in enumerate((wg, wv, wd)):
                p.mm(g["ps%d" % wi][:, 0:TT],
                     [(wt[:, kc, oc * 128:(oc + 1) * 128], xn[:, kc, :])
                      for kc in range(8)])
            f2 = SA[:, 24576:25088]
            sg = SA[:, 25088:25600]
            p.op("scalar", "activation", out=f2, in_=g["ps0"][:, 0:TT],
                 func=ACT.Sigmoid, bias=SM[:, oc:oc + 1],
                 reads=[g["ps0"][:, 0:TT], SM[:, oc:oc + 1]], writes=[f2])
            p.op("scalar", "activation", out=sg, in_=g["ps1"][:, 0:TT],
                 func=ACT.Tanh, bias=SM[:, 8 + oc:9 + oc],
                 reads=[g["ps1"][:, 0:TT], SM[:, 8 + oc:9 + oc]], writes=[sg])
            p.op("vector", "tensor_tensor", out=xscan[:, oc, :], in0=f2,
                 in1=sg, op=ALU.mult, reads=[f2, sg], writes=[xscan[:, oc, :]])
            p.op("scalar", "activation", out=f2, in_=g["ps2"][:, 0:TT],
                 func=ACT.Sigmoid, bias=SM[:, 16 + oc:17 + oc],
                 reads=[g["ps2"][:, 0:TT], SM[:, 16 + oc:17 + oc]], writes=[f2])
            p.op("vector", "tensor_scalar", out=av[:, oc, :], in0=f2,
                 scalar1=0.998, scalar2=0.001, op0=ALU.mult, op1=ALU.add,
                 reads=[f2], writes=[av[:, oc, :]])
        p.dma("qG", chv(g["s1_xscan"], tt), xscan)
        p.dma("qG", chv(g["s1_a"], tt), av)

    ones_t = SA[:, 4 * p.tpc:5 * p.tpc]
    p.op("vector", "memset", ones_t, 1.0, writes=[ones_t])
    for ch in range(8):
        xsv = SA[:, 0:p.tpc]
        avv = SA[:, p.tpc:2 * p.tpc]
        hv = SA[:, 2 * p.tpc:3 * p.tpc]
        pv = SA[:, 3 * p.tpc:4 * p.tpc]
        p.dma("qS", xsv, g["s1_xscan"].rearrange("(c p) t -> p c t", p=128)[:, ch, :])
        p.dma("qS", avv, g["s1_a"].rearrange("(c p) t -> p c t", p=128)[:, ch, :])
        p.op("vector", "tensor_tensor_scan", out=hv, data0=avv, data1=xsv,
             initial=0.0, op0=ALU.mult, op1=ALU.add,
             reads=[avv, xsv], writes=[hv])
        p.op("vector", "tensor_tensor_scan", out=pv, data0=avv, data1=ones_t,
             initial=1.0, op0=ALU.mult, op1=ALU.mult,
             reads=[avv, ones_t], writes=[pv])
        p.op("vector", "tensor_copy", out=SM[:, 48 + ch:49 + ch],
             in_=pv[:, p.tpc - 1:p.tpc],
             reads=[pv[:, p.tpc - 1:p.tpc]], writes=[SM[:, 48 + ch:49 + ch]])
        p.op("vector", "tensor_copy", out=SM[:, 56 + ch:57 + ch],
             in_=hv[:, p.tpc - 1:p.tpc],
             reads=[hv[:, p.tpc - 1:p.tpc]], writes=[SM[:, 56 + ch:57 + ch]])
        p.dma("qG", g["s1_h"].rearrange("(c p) t -> p c t", p=128)[:, ch, :], hv)
        p.dma("qG", g["s1_P"].rearrange("(c p) t -> p c t", p=128)[:, ch, :], pv)

    p.dma("qG", g["cc_s1_in"][0, :].rearrange("(c p) -> p c", p=128),
          SM[:, 48:56], allow_slow_non_contiguous=True)
    p.dma("qG", g["cc_s1_in"][1, :].rearrange("(c p) -> p c", p=128),
          SM[:, 56:64], allow_slow_non_contiguous=True)
    p.coll("AllGather", ALU.bypass, [[0, 1, 2, 3], [4, 5, 6, 7]],
           [g["cc_s1_in"]], [g["cc_s1_out"]])
    for j in range(4):
        p.dma("qS", SM[:, 64 + 16 * j:72 + 16 * j],
              g["cc_s1_out"][j, 0, :].rearrange("(c p) -> p c", p=128),
              allow_slow_non_contiguous=True)
        p.dma("qS", SM[:, 72 + 16 * j:80 + 16 * j],
              g["cc_s1_out"][j, 1, :].rearrange("(c p) -> p c", p=128),
              allow_slow_non_contiguous=True)
    p.op("vector", "memset", SM[:, 128:136], 0.0, writes=[SM[:, 128:136]])
    for j in range(4):
        mj = SM[:, 40 + j:41 + j]
        cj = SM[:, 44 + j:45 + j]
        tm = SM[:, 136:144]
        p.op("vector", "tensor_scalar", out=tm, in0=SM[:, 64 + 16 * j:72 + 16 * j],
             scalar1=mj, scalar2=cj, op0=ALU.mult, op1=ALU.add,
             reads=[SM[:, 64 + 16 * j:72 + 16 * j], mj, cj], writes=[tm])
        p.op("vector", "tensor_tensor", out=SM[:, 128:136], in0=SM[:, 128:136],
             in1=tm, op=ALU.mult, reads=[SM[:, 128:136], tm],
             writes=[SM[:, 128:136]])
        p.op("vector", "tensor_scalar", out=tm, in0=SM[:, 72 + 16 * j:80 + 16 * j],
             scalar1=mj, scalar2=None, op0=ALU.mult,
             reads=[SM[:, 72 + 16 * j:80 + 16 * j], mj], writes=[tm])
        p.op("vector", "tensor_add", out=SM[:, 128:136], in0=SM[:, 128:136],
             in1=tm, reads=[SM[:, 128:136], tm], writes=[SM[:, 128:136]])

    for tt in range(ntt):
        o = 0
        hv = SA[:, o:o + 4096].rearrange("p (c t) -> p c t", c=8)
        pv = SA[:, o + 4096:o + 8192].rearrange("p (c t) -> p c t", c=8)
        lv = SA[:, o + 8192:o + 12288].rearrange("p (c t) -> p c t", c=8)
        xv = SA[:, o + 12288:o + 16384].rearrange("p (c t) -> p c t", c=8)
        p.dma("qS", hv, chv(g["s1_h"], tt))
        p.dma("qS", pv, chv(g["s1_P"], tt))
        p.dma("qS", lv, chv(g["ln0f"], tt))
        p.dma("qS", xv, chv(g["xs0"], tt))
        for ch in range(8):
            Hc = SM[:, 128 + ch:129 + ch]
            p.op("vector", "scalar_tensor_tensor", out=hv[:, ch, :],
                 in0=pv[:, ch, :], scalar=Hc, in1=hv[:, ch, :],
                 op0=ALU.mult, op1=ALU.add,
                 reads=[pv[:, ch, :], Hc, hv[:, ch, :]], writes=[hv[:, ch, :]])
            p.op("vector", "tensor_add", out=lv[:, ch, :], in0=lv[:, ch, :],
                 in1=hv[:, ch, :], reads=[lv[:, ch, :], hv[:, ch, :]],
                 writes=[lv[:, ch, :]])
            p.op("vector", "tensor_add", out=xv[:, ch, :], in0=xv[:, ch, :],
                 in1=lv[:, ch, :], reads=[xv[:, ch, :], lv[:, ch, :]],
                 writes=[xv[:, ch, :]])
        sq = pv
        for ch in range(8):
            p.op("scalar", "activation", out=sq[:, ch, :], in_=lv[:, ch, :],
                 func=ACT.Square, reads=[lv[:, ch, :]], writes=[sq[:, ch, :]])
        colsum(p, g["ps0"][0:1, 0:TT], [sq[:, ch, :] for ch in range(8)])
        r1 = ST[0:1, 0:TT]
        p.op("vector", "tensor_scalar", out=r1, in0=g["ps0"][0:1, 0:TT],
             scalar1=1.0 / D, scalar2=None, op0=ALU.mult,
             reads=[g["ps0"][0:1, 0:TT]], writes=[r1])
        r2 = ST[0:1, TT:2 * TT]
        p.op("scalar", "activation", out=r2, in_=r1, func=ACT.Sqrt,
             bias=p.cst(1e-6, 1), reads=[r1], writes=[r2])
        p.op("vector", "reciprocal", out=r2, in_=r2, reads=[r2], writes=[r2])
        bcast(p, g["ps4"], r2)
        yn = SH[:, (tt % 2) * 4096:(tt % 2) * 4096 + 4096].rearrange(
            "p (c t) -> p c t", c=8)
        ynf = sq
        for ch in range(8):
            wffn = SM[:, 32 + ch:33 + ch]
            p.op("vector", "scalar_tensor_tensor", out=ynf[:, ch, :],
                 in0=lv[:, ch, :], scalar=wffn, in1=g["ps4"][:, 0:TT],
                 op0=ALU.mult, op1=ALU.mult,
                 reads=[lv[:, ch, :], wffn, g["ps4"][:, 0:TT]],
                 writes=[ynf[:, ch, :]])
            p.op("scalar", "copy", out=yn[:, ch, :], in_=ynf[:, ch, :],
                 reads=[ynf[:, ch, :]], writes=[yn[:, ch, :]])
        p.dma("qG", chv(g["s1_xpi"], tt), xv)
        p.dma("qG", chv(g["s1_yn"], tt), yn)

    for half in range(2):
        hw = p.tpc // 2
        W1 = min(TT, hw)
        ynh = SH[:, 0:8 * hw].rearrange("p (c t) -> p c t", c=8)
        p.dma("qS", ynh, chv(g["s1_yn"], None)[:, :, half * hw:(half + 1) * hw])
        for oc in range(24):
            par = oc % 2
            wgt = load_w_lhsT(p, g["h_Wgate"][:, oc * 128:(oc + 1) * 128],
                              16384 + par * 2048, 8, 128)
            wut = load_w_lhsT(p, g["h_Wup"][:, oc * 128:(oc + 1) * 128],
                              16384 + par * 2048 + 1024, 8, 128)
            for ts in range(hw // W1):
                pg = g["ps%d" % (ts % 2)][:, 0:W1]
                pu = g["ps%d" % (2 + ts % 2)][:, 0:W1]
                p.mm(pg, [(wgt[:, kc, :], ynh[:, kc, ts * W1:(ts + 1) * W1])
                          for kc in range(8)])
                p.mm(pu, [(wut[:, kc, :], ynh[:, kc, ts * W1:(ts + 1) * W1])
                          for kc in range(8)])
                sl = SA[:, (ts % 2) * 512:(ts % 2) * 512 + W1]
                p.op("scalar", "activation", out=sl, in_=pg, func=ACT.Silu,
                     reads=[pg], writes=[sl])
                hb = SH[:, 20480 + (ts % 2) * 512:20480 + (ts % 2) * 512 + W1]
                p.op("vector", "tensor_tensor", out=hb, in0=sl, in1=pu,
                     op=ALU.mult, reads=[sl, pu], writes=[hb])
                p.dma("qG", g["s1_hact"][oc * 128:(oc + 1) * 128,
                                         half * hw + ts * W1:
                                         half * hw + (ts + 1) * W1], hb)
    wo = load_w_lhsT(p, g["h_Wout"], 0, 24, 1024)
    W2 = 256
    for t2 in range(p.tpc // W2):
        par = t2 % 2
        hv = SH[:, 24576:24576 + 24 * W2].rearrange("p (kc t) -> p kc t", kc=24)
        hsrc = g["s1_hact"].rearrange("(kc p) t -> p kc t", p=128)
        for k0 in range(0, 24, 8):
            p.dma("qS", hv[:, k0:k0 + 8, :],
                  hsrc[:, k0:k0 + 8, t2 * W2:(t2 + 1) * W2])
        xpv = SA[:, par * 4096:par * 4096 + 2048].rearrange(
            "p (c t) -> p c t", c=8)
        p.dma("qS", xpv, chv(g["s1_xpi"], t2, W2))
        ov = SA[:, par * 4096 + 2048:par * 4096 + 4096].rearrange(
            "p (c t) -> p c t", c=8)
        for och in range(8):
            ps = g["ps%d" % och][:, 0:W2]
            p.mm(ps, [(wo[:, kc, och * 128:(och + 1) * 128], hv[:, kc, :])
                      for kc in range(24)])
            p.op("vector", "tensor_add", out=ov[:, och, :], in0=xpv[:, och, :],
                 in1=ps, reads=[xpv[:, och, :], ps], writes=[ov[:, och, :]])
        p.dma("qG", chv(g["xs1"], t2, W2), ov)


def emit_s2(p: P, src, dst):
    g = p.g
    ntt = p.ntt
    SA, SH, ST, SM, SHB = g["SA"], g["SH"], g["ST"], g["SM"], g["SHB"]
    CF = g["CF"]
    E = 16
    emit_ln(p, g[src], [(g["moe_lnf"], F32), (g["moe_lnh"], BF16)])

    gw = SA[:, 26624:26752].rearrange("p (kc m) -> p kc m", kc=8)
    p.dma("qS", gw, g["moe_gwT"].rearrange("(kc p) m -> p kc m", p=128))
    b16 = SA[0:1, 26752:26768]
    p.dma("qS", b16, g["moe_gb"])
    bcast(p, g["ps6"][:, 0:16], b16)
    BIAS16 = SA[:, 26768:26784]
    p.op("vector", "tensor_copy", out=BIAS16, in_=g["ps6"][:, 0:16],
         reads=[g["ps6"][:, 0:16]], writes=[BIAS16])
    comb_cm = SA[:, 16384:16384 + p.tpc]
    R = 20480
    for tt in range(ntt):
        xv = SA[:, (tt % 2) * 8192:(tt % 2) * 8192 + 4096].rearrange(
            "p (c t) -> p c t", c=8)
        p.dma("qS", xv, chv(g["moe_lnf"], tt))
        lg = g["ps5"][:, 0:TT]
        p.mm(lg[0:16, :], [(gw[:, kc, :], xv[:, kc, :]) for kc in range(8)])
        sc_cm = SA[:, R + 512:R + 1024][0:16, 0:TT]
        p.op("scalar", "activation", out=sc_cm, in_=lg[0:16, :],
             func=ACT.Sigmoid, reads=[lg[0:16, :]], writes=[sc_cm])
        for st4 in range(TT // 128):
            t0 = st4 * 128
            p.op("tensor", "transpose", out=g["ps6"][:, 0:16],
                 in_=sc_cm[0:16, t0:t0 + 128], identity=CF[0:16, 132:148],
                 reads=[sc_cm[0:16, t0:t0 + 128], CF[0:16, 132:148]],
                 writes=[g["ps6"][:, 0:16]])
            sc = SA[:, R + 1024:R + 1040]
            p.op("vector", "tensor_copy", out=sc, in_=g["ps6"][:, 0:16],
                 reads=[g["ps6"][:, 0:16]], writes=[sc])
            sb = SA[:, R + 1040:R + 1056]
            p.op("vector", "tensor_tensor", out=sb, in0=sc, in1=BIAS16,
                 op=ALU.add, reads=[sc, BIAS16], writes=[sb])
            cols = SA[:, R + 1056:R + 1072]
            m1s, m2s = cols[:, 0:4], cols[:, 4:8]
            cr, cd, mc = cols[:, 8:9], cols[:, 9:10], cols[:, 10:11]
            R2 = SA[:, R + 1072:R + 1088]
            G1 = SA[:, R + 1088:R + 1092]
            G3 = SA[:, R + 1092:R + 1096]
            R3 = SA[:, R + 1096:R + 1112]
            R4 = SA[:, R + 1112:R + 1128]
            for gg in range(4):
                sbg = sb[:, 4 * gg:4 * gg + 4]
                m1 = m1s[:, gg:gg + 1]
                p.op("vector", "tensor_reduce", out=m1, in_=sbg, axis=AXX,
                     op=ALU.max, reads=[sbg], writes=[m1])
                eq = R2[:, 4 * gg:4 * gg + 4]
                p.op("vector", "tensor_scalar", out=eq, in0=sbg, scalar1=m1,
                     scalar2=None, op0=ALU.is_ge, reads=[sbg, m1], writes=[eq])
                p.op("vector", "scalar_tensor_tensor", out=eq, in0=eq,
                     scalar=-BIG, in1=sbg, op0=ALU.mult, op1=ALU.add,
                     reads=[eq, sbg], writes=[eq])
                p.op("vector", "tensor_reduce", out=m2s[:, gg:gg + 1], in_=eq,
                     axis=AXX, op=ALU.max, reads=[eq],
                     writes=[m2s[:, gg:gg + 1]])
            p.op("vector", "tensor_tensor", out=G1, in0=m1s, in1=m2s,
                 op=ALU.add, reads=[m1s, m2s], writes=[G1])
            p.op("vector", "tensor_reduce", out=cr, in_=G1, axis=AXX,
                 op=ALU.max, reads=[G1], writes=[cr])
            p.op("vector", "tensor_scalar", out=G3, in0=G1, scalar1=cr,
                 scalar2=None, op0=ALU.is_ge, reads=[G1, cr], writes=[G3])
            p.op("vector", "scalar_tensor_tensor", out=G3, in0=G3, scalar=-BIG,
                 in1=G1, op0=ALU.mult, op1=ALU.add, reads=[G3, G1], writes=[G3])
            p.op("vector", "tensor_reduce", out=cd, in_=G3, axis=AXX,
                 op=ALU.max, reads=[G3], writes=[cd])
            p.op("vector", "tensor_scalar", out=G3, in0=G1, scalar1=cd,
                 scalar2=None, op0=ALU.is_ge, reads=[G1, cd], writes=[G3])
            for gg in range(4):
                gcol = G3[:, gg:gg + 1]
                p.op("vector", "tensor_scalar", out=mc, in0=gcol, scalar1=-1.0,
                     scalar2=BIG, op0=ALU.add, op1=ALU.mult,
                     reads=[gcol], writes=[mc])
                p.op("vector", "tensor_scalar", out=R3[:, 4 * gg:4 * gg + 4],
                     in0=sb[:, 4 * gg:4 * gg + 4], scalar1=gcol, scalar2=None,
                     op0=ALU.mult, reads=[sb[:, 4 * gg:4 * gg + 4], gcol],
                     writes=[R3[:, 4 * gg:4 * gg + 4]])
                p.op("vector", "tensor_scalar", out=R3[:, 4 * gg:4 * gg + 4],
                     in0=R3[:, 4 * gg:4 * gg + 4], scalar1=mc, scalar2=None,
                     op0=ALU.add, reads=[R3[:, 4 * gg:4 * gg + 4], mc],
                     writes=[R3[:, 4 * gg:4 * gg + 4]])
            p.op("vector", "tensor_copy", out=R4, in_=R3, reads=[R3],
                 writes=[R4])
            p.op("vector", "tensor_reduce", out=cr, in_=R3, axis=AXX,
                 op=ALU.max, reads=[R3], writes=[cr])
            for rnd in range(3):
                p.op("vector", "tensor_scalar", out=R2, in0=R3, scalar1=cr,
                     scalar2=None, op0=ALU.is_ge, reads=[R3, cr], writes=[R2])
                p.op("vector", "scalar_tensor_tensor", out=R3, in0=R2,
                     scalar=-BIG, in1=R3, op0=ALU.mult, op1=ALU.add,
                     reads=[R2, R3], writes=[R3])
                p.op("vector", "tensor_reduce", out=cr, in_=R3, axis=AXX,
                     op=ALU.max, reads=[R3], writes=[cr])
            p.op("vector", "tensor_scalar", out=R2, in0=R4, scalar1=cr,
                 scalar2=None, op0=ALU.is_ge, reads=[R4, cr], writes=[R2])
            p.op("vector", "tensor_tensor", out=R2, in0=R2, in1=sc,
                 op=ALU.mult, reads=[R2, sc], writes=[R2])
            p.op("vector", "tensor_reduce", out=cd, in_=R2, axis=AXX,
                 op=ALU.add, reads=[R2], writes=[cd])
            p.op("vector", "tensor_scalar", out=cd, in0=cd, scalar1=1e-20,
                 scalar2=None, op0=ALU.add, reads=[cd], writes=[cd])
            p.op("vector", "reciprocal", out=cd, in_=cd, reads=[cd],
                 writes=[cd])
            p.op("vector", "tensor_scalar", out=R2, in0=R2, scalar1=cd,
                 scalar2=2.5, op0=ALU.mult, op1=ALU.mult,
                 reads=[R2, cd], writes=[R2])
            p.op("tensor", "transpose", out=g["ps6"][0:16, 16:144],
                 in_=R2, identity=CF[:, 132:260],
                 reads=[R2, CF[:, 132:260]], writes=[g["ps6"][0:16, 16:144]])
            dstc = comb_cm[0:16, tt * TT + t0:tt * TT + t0 + 128]
            p.op("vector", "tensor_copy", out=dstc,
                 in_=g["ps6"][0:16, 16:144],
                 reads=[g["ps6"][0:16, 16:144]], writes=[dstc])

    p.dma("qG", g["moe_comb"], comb_cm[0:16, 0:p.tpc])

    for e in range(E + 1):
        if e < E:
            w1s = g["h_w1T"][e * 1024:(e + 1) * 1024, :]
            w3s = g["h_w3T"][e * 1024:(e + 1) * 1024, :]
        else:
            w1s, w3s = g["h_ws1T"], g["h_ws3T"]
        par = e % 2
        w1t = load_w_lhsT(p, w1s, par * 12288, 8, 768)
        w3t = load_w_lhsT(p, w3s, par * 12288 + 6144, 8, 768)
        for tt in range(ntt):
            ynt = SH[:, 24576 + (tt % 2) * 4096:24576 + (tt % 2) * 4096 + 4096] \
                .rearrange("p (c t) -> p c t", c=8)
            p.dma("qS", ynt, chv(g["moe_lnh"], tt))
            if e < E:
                crow = SA[0:1, 26784:26784 + TT]
                p.dma("qS", crow, g["moe_comb"][e:e + 1, tt * TT:(tt + 1) * TT])
                bcast(p, g["ps4"], crow)
            for oc in range(6):
                pg = g["ps%d" % (2 * (oc % 2))][:, 0:TT]
                pu = g["ps%d" % (2 * (oc % 2) + 1)][:, 0:TT]
                p.mm(pg, [(w1t[:, kc, oc * 128:(oc + 1) * 128], ynt[:, kc, :])
                          for kc in range(8)])
                p.mm(pu, [(w3t[:, kc, oc * 128:(oc + 1) * 128], ynt[:, kc, :])
                          for kc in range(8)])
                sl = SA[:, (oc % 2) * 512:(oc % 2) * 512 + 512]
                p.op("scalar", "activation", out=sl, in_=pg, func=ACT.Silu,
                     reads=[pg], writes=[sl])
                p.op("vector", "tensor_tensor", out=sl, in0=sl, in1=pu,
                     op=ALU.mult, reads=[sl, pu], writes=[sl])
                if e < E:
                    p.op("vector", "tensor_tensor", out=sl, in0=sl,
                         in1=g["ps4"][:, 0:TT], op=ALU.mult,
                         reads=[sl, g["ps4"][:, 0:TT]], writes=[sl])
                hb = SHB[:, (oc % 2) * 512:(oc % 2) * 512 + 512]
                p.op("scalar", "copy", out=hb, in_=sl, reads=[sl], writes=[hb])
                p.dma("qG", g["moe_hact"][e * 768 + oc * 128:
                                          e * 768 + (oc + 1) * 128,
                                          tt * TT:(tt + 1) * TT], hb)

    WMT = min(p.tpc, 2048)
    nmt = p.tpc // WMT
    for mt in range(nmt):
        y = SA[:, 0:8 * WMT].rearrange("p (c t) -> p c t", c=8)
        p.dma("qS", y, chv(g[src], None)[:, :, mt * WMT:(mt + 1) * WMT])
        for e in range(E + 1):
            w2s = g["h_w2T"][e * 768:(e + 1) * 768, :] if e < E else g["h_ws2T"]
            w2t = load_w_lhsT(p, w2s, 24576, 6, 1024)
            ha = SH[:, (e % 2) * 12288:(e % 2) * 12288 + 6 * WMT].rearrange(
                "p (kc t) -> p kc t", kc=6)
            p.dma("qS", ha,
                  g["moe_hact"][e * 768:(e + 1) * 768, :]
                  .rearrange("(kc p) t -> p kc t", p=128)
                  [:, :, mt * WMT:(mt + 1) * WMT])
            for och in range(8):
                for ts in range(WMT // TT):
                    ps = g["ps%d" % (och % 4)][:, 0:TT]
                    p.mm(ps, [(w2t[:, kc, och * 128:(och + 1) * 128],
                               ha[:, kc, ts * TT:(ts + 1) * TT])
                              for kc in range(6)])
                    yv = y[:, och, ts * TT:(ts + 1) * TT]
                    p.op("vector", "tensor_add", out=yv, in0=yv, in1=ps,
                         reads=[yv, ps], writes=[yv])
        p.dma("qG", chv(g[dst], None)[:, :, mt * WMT:(mt + 1) * WMT], y)


def emit_s3(p: P):
    g = p.g
    ntt = p.ntt
    ntt4 = ntt // 4
    S4 = 4 * p.tpc
    SA, SH, ST, SM, CF = g["SA"], g["SH"], g["ST"], g["SM"], g["CF"]
    GRPS = [[0, 1, 2, 3], [4, 5, 6, 7]]

    emit_ln(p, g["xs2"], [(g["s3_lnh"], BF16)])
    p.coll("AllGather", ALU.bypass, GRPS, [g["s3_lnh"]], [g["s3_ynall"]])
    p.dma("qS", SM[:, 160:164], g["qsel"])

    p.dma("qS", SM[:, 144:152], g["l0b_bq"].rearrange("(c p) -> p c", p=128),
          allow_slow_non_contiguous=True)
    p.dma("qS", SM[:, 152:160], g["l0b_bk"].rearrange("(c p) -> p c", p=128),
          allow_slow_non_contiguous=True)
    vbrow = SA[0:1, 26624:27648]
    p.dma("qS", vbrow, g["l0b_bv"])
    VB = SA[:, 25600:26624]
    for hh in range(2):
        bcast(p, g["ps6"], vbrow[0:1, hh * 512:(hh + 1) * 512])
        p.op("vector", "tensor_copy", out=VB[:, hh * 512:(hh + 1) * 512],
             in_=g["ps6"][:, 0:512], reads=[g["ps6"][:, 0:512]],
             writes=[VB[:, hh * 512:(hh + 1) * 512]])

    wq = g["h_l0bW"][:, 0:1024]
    wk = g["h_l0bW"][:, 1024:2048]
    wvr = g["h_l0bW"][:, 2048:3072]
    wkt = load_w_lhsT(p, wk, 0, 8, 1024)
    wvt = SH[:, 8192:16384].rearrange("p (kc m) -> p kc m", kc=8)
    p.dma("qS", wvt, wvr.rearrange("(kc p) m -> p kc m", p=128))
    for gt in range(4 * ntt):
        yv = SH[:, 16384 + (gt % 2) * 4096:16384 + (gt % 2) * 4096 + 4096] \
            .rearrange("p (c t) -> p c t", c=8)
        p.dma("qS", yv, g["s3_ynall"][gt // ntt].rearrange(
            "(c p) t -> p c t", p=128)
            [:, :, (gt % ntt) * TT:(gt % ntt + 1) * TT])
        ko = SH[:, 24576 + (gt % 2) * 4096:24576 + (gt % 2) * 4096 + 4096] \
            .rearrange("p (c t) -> p c t", c=8)
        for oc in range(8):
            ps = g["ps%d" % (oc % 4)][:, 0:TT]
            p.mm(ps, [(wkt[:, kc, oc * 128:(oc + 1) * 128], yv[:, kc, :])
                      for kc in range(8)])
            p.op("vector", "tensor_scalar", out=ko[:, oc, :], in0=ps,
                 scalar1=SM[:, 152 + oc:153 + oc], scalar2=None, op0=ALU.add,
                 reads=[ps, SM[:, 152 + oc:153 + oc]], writes=[ko[:, oc, :]])
        p.dma("qG", chv(g["s3_k"], gt), ko)
        for t4 in range(4):
            vo = g["SHB"][:, 0:1024]
            for dvh in range(2):
                ps = g["ps%d" % (4 + dvh)][:, 0:TT]
                p.mm(ps, [(yv[:, kc, t4 * 128:(t4 + 1) * 128],
                           wvt[:, kc, dvh * 512:(dvh + 1) * 512])
                          for kc in range(8)])
                p.op("vector", "tensor_tensor",
                     out=vo[:, dvh * 512:(dvh + 1) * 512], in0=ps,
                     in1=VB[:, dvh * 512:(dvh + 1) * 512], op=ALU.add,
                     reads=[ps, VB[:, dvh * 512:(dvh + 1) * 512]],
                     writes=[vo[:, dvh * 512:(dvh + 1) * 512]])
            p.dma("qG", g["s3_v"][gt * TT + t4 * 128:gt * TT + (t4 + 1) * 128, :],
                  vo)

    wqt = load_w_lhsT(p, wq, 0, 8, 1024)
    for m in range(ntt):
        yv = SH[:, 16384 + (m % 2) * 4096:16384 + (m % 2) * 4096 + 4096] \
            .rearrange("p (c t) -> p c t", c=8)
        p.op("vector", "memset", yv[:, :, :], 0.0, writes=[yv[:, :, :]])
        for cnd in range(4):
            s_g = 4 * m + cnd
            hh = s_g // ntt
            lc = s_g * TT - hh * p.tpc
            cb = SH[:, 8192 + (cnd % 2) * 4096:8192 + (cnd % 2) * 4096 + 4096] \
                .rearrange("p (c t) -> p c t", c=8)
            p.dma("qS", cb, g["s3_ynall"][hh].rearrange(
                "(c p) t -> p c t", p=128)[:, :, lc:lc + TT])
            qc = SM[:, 160 + cnd:161 + cnd]
            for ch in range(8):
                p.op("vector", "scalar_tensor_tensor", out=yv[:, ch, :],
                     in0=cb[:, ch, :], scalar=qc, in1=yv[:, ch, :],
                     op0=ALU.mult, op1=ALU.add,
                     reads=[cb[:, ch, :], qc, yv[:, ch, :]],
                     writes=[yv[:, ch, :]])
        qo = SH[:, 24576 + (m % 2) * 4096:24576 + (m % 2) * 4096 + 4096] \
            .rearrange("p (c t) -> p c t", c=8)
        for oc in range(8):
            ps = g["ps%d" % (oc % 4)][:, 0:TT]
            p.mm(ps, [(wqt[:, kc, oc * 128:(oc + 1) * 128], yv[:, kc, :])
                      for kc in range(8)])
            p.op("vector", "tensor_scalar", out=qo[:, oc, :], in0=ps,
                 scalar1=SM[:, 144 + oc:145 + oc], scalar2=None, op0=ALU.add,
                 reads=[ps, SM[:, 144 + oc:145 + oc]], writes=[qo[:, oc, :]])
        p.dma("qG", chv(g["s3_q"], m), qo)

    th_i = 0
    crc = SA[:, 25088:25600]
    p.dma("qS", crc, g["s3_cr"])
    n_theta = 8 * ntt * (ntt + 1)
    tha = SA[:, 27008:27008 + n_theta]
    p.dma("qS", tha, g["s3_theta"])
    osb = SA[:, 16384:16384 + 4096].rearrange("p (c t) -> p c t", c=8)
    for m in range(ntt):
        qv = SH[:, 0:4096].rearrange("p (c t) -> p c t", c=8)
        p.dma("qS", qv, chv(g["s3_q"], m))
        p.op("vector", "memset", osb[:, :, :], 0.0, writes=[osb[:, :, :]])
        Lm = 16 * m + 16
        ngr = Lm // 8
        first = True
        for gr in range(ngr):
            pb = SH[:, 4096 + (gr % 2) * 4096:4096 + (gr % 2) * 4096 + 4096]
            vb = SH[:, 12288 + (gr % 2) * 8192:12288 + (gr % 2) * 8192 + 8192]
            for t8 in range(8):
                kt = gr * 8 + t8
                kv = SH[:, 28672 + (t8 % 2) * 2048:28672 + (t8 % 2) * 2048 + 1024] \
                    .rearrange("p (c t) -> p c t", c=8)
                p.dma("qS", kv, chv(g["s3_k"], kt, 128))
                p.dma("qS", vb[:, t8 * 1024:(t8 + 1) * 1024],
                      g["s3_v"][kt * 128:(kt + 1) * 128, :])
                pss = g["ps%d" % (5 + t8 % 2)][:, 0:TT]
                p.mm(pss, [(kv[:, kc, :], qv[:, kc, :]) for kc in range(8)])
                pf = SA[:, (t8 % 2) * 512:(t8 % 2) * 512 + 512]
                p.op("scalar", "activation", out=pf, in_=pss, func=ACT.Exp,
                     scale=1.0 / 32.0, reads=[pss], writes=[pf])
                msk = SA[:, 1024 + (t8 % 2) * 512:1024 + (t8 % 2) * 512 + 512]
                th = tha[:, th_i:th_i + 1]
                p.op("vector", "tensor_scalar", out=msk, in0=crc, scalar1=th,
                     scalar2=None, op0=ALU.is_ge, reads=[crc, th], writes=[msk])
                th_i += 1
                p.op("vector", "tensor_tensor", out=pb[:, t8 * 512:(t8 + 1) * 512],
                     in0=pf, in1=msk, op=ALU.mult,
                     reads=[pf, msk], writes=[pb[:, t8 * 512:(t8 + 1) * 512]])
                p.mm(g["ps7"][0:1, 0:TT],
                     [(g["CH"][:, 128:129], pb[:, t8 * 512:(t8 + 1) * 512])],
                     cont=(not first), stop=(kt == Lm - 1))
                first = False
            pbv = pb.rearrange("p (n t) -> p n t", n=8)
            vbv = vb.rearrange("p (n t) -> p n t", n=8)
            for half in range(2):
                for dvc in range(4):
                    ps = g["ps%d" % dvc][:, 0:TT]
                    p.mm(ps, [(vbv[:, t8, (half * 4 + dvc) * 128:
                                   (half * 4 + dvc + 1) * 128],
                               pbv[:, t8, :]) for t8 in range(8)])
                    ov = osb[:, half * 4 + dvc, :]
                    p.op("vector", "tensor_add", out=ov, in0=ov, in1=ps,
                         reads=[ov, ps], writes=[ov])
        lrow = ST[0:1, 0:TT]
        p.op("vector", "tensor_copy", out=lrow, in_=g["ps7"][0:1, 0:TT],
             reads=[g["ps7"][0:1, 0:TT]], writes=[lrow])
        p.op("vector", "reciprocal", out=lrow, in_=lrow, reads=[lrow],
             writes=[lrow])
        bcast(p, g["ps6"], lrow)
        oh = SH[:, 0:4096].rearrange("p (c t) -> p c t", c=8)
        for ch in range(8):
            p.op("vector", "tensor_tensor", out=osb[:, ch, :],
                 in0=osb[:, ch, :], in1=g["ps6"][:, 0:TT], op=ALU.mult,
                 reads=[osb[:, ch, :], g["ps6"][:, 0:TT]],
                 writes=[osb[:, ch, :]])
            p.op("scalar", "copy", out=oh[:, ch, :], in_=osb[:, ch, :],
                 reads=[osb[:, ch, :]], writes=[oh[:, ch, :]])
        p.dma("qG", g["s3_oin"][m].rearrange("(c p) t -> p c t", p=128), oh)

    p.coll("AllGather", ALU.bypass, GRPS, [g["s3_oin"]], [g["s3_oout"]])
    for tt in range(ntt):
        xv = SA[:, (tt % 2) * 8192:(tt % 2) * 8192 + 4096].rearrange(
            "p (c t) -> p c t", c=8)
        p.dma("qS", xv, chv(g["xs2"], tt))
        src_j = tt % 4
        for cnd in range(4):
            mslot = cnd * ntt4 + tt // 4
            ob = SH[:, (cnd % 2) * 4096:(cnd % 2) * 4096 + 4096].rearrange(
                "p (c t) -> p c t", c=8)
            p.dma("qS", ob, g["s3_oout"][src_j, mslot].rearrange(
                "(c p) t -> p c t", p=128))
            qc = SM[:, 160 + cnd:161 + cnd]
            for ch in range(8):
                p.op("vector", "scalar_tensor_tensor", out=xv[:, ch, :],
                     in0=ob[:, ch, :], scalar=qc, in1=xv[:, ch, :],
                     op0=ALU.mult, op1=ALU.add,
                     reads=[ob[:, ch, :], qc, xv[:, ch, :]],
                     writes=[xv[:, ch, :]])
        p.dma("qG", chv(g["xs3"], tt), xv)


def emit_s5(p: P, src, dst):
    g = p.g
    ntt = p.ntt
    SA, SH, ST, SM, CF, SHB = g["SA"], g["SH"], g["ST"], g["SM"], g["CF"], g["SHB"]
    GRPS = [[0, 1, 2, 3], [4, 5, 6, 7]]
    HW = 48
    EXT = p.tpc + HW
    IDY = CF[:, 132:260]

    emit_ln(p, g[src], [(g["s5_lnh"], BF16)])
    p.dma("qS", SM[:, 164:168], g["chsel"])
    p.dma("qS", SM[:, 200:204], g["prevsel"])

    p.dma("qG", g["s5_hin"], g["s5_lnh"][:, p.tpc - HW:p.tpc])
    p.coll("AllGather", ALU.bypass, GRPS, [g["s5_hin"]], [g["s5_hout"]])
    p.dma("qG", g["s5_xE"][:, HW:EXT], g["s5_lnh"])
    hb = SH[:, 0:8 * HW].rearrange("p (c t) -> p c t", c=8)
    p.op("vector", "memset", hb[:, :, :], 0.0, writes=[hb[:, :, :]])
    for cnd in range(4):
        cb = SH[:, 1024 + cnd * 512:1024 + cnd * 512 + 8 * HW].rearrange(
            "p (c t) -> p c t", c=8)
        p.dma("qS", cb, g["s5_hout"][cnd].rearrange("(c p) t -> p c t", p=128))
        pc = SM[:, 200 + cnd:201 + cnd]
        for ch in range(8):
            p.op("vector", "scalar_tensor_tensor", out=hb[:, ch, :],
                 in0=cb[:, ch, :], scalar=pc, in1=hb[:, ch, :],
                 op0=ALU.mult, op1=ALU.add,
                 reads=[cb[:, ch, :], pc, hb[:, ch, :]], writes=[hb[:, ch, :]])
    p.dma("qG", g["s5_xE"][:, 0:HW].rearrange("(c p) t -> p c t", p=128), hb)

    for nm, c0, w in (("l1a_qcw", 204, 32), ("l1a_kcw", 236, 32),
                      ("l1a_vcw", 268, 32), ("l1a_onw", 300, 2),
                      ("l1a_f3", 304, 24), ("l1a_f7", 328, 56),
                      ("l1a_f15", 384, 120), ("l1a_f31", 504, 248)):
        p.dma("qS", SM[:, c0:c0 + w], g[nm])
    p.dma("qS", SM[0:4, 752:753], g["l1a_idb"])
    p.dma("qS", SM[:, 753:754], g["l1a_rb2"])
    p.dma("qS", SM[:, 754:770], g["l1a_rb1"].rearrange("(c p) -> p c", p=128),
          allow_slow_non_contiguous=True)

    if S5_CUT < 1:
        return
    for fam, wname, dstp in (("q", "h_Wq", "s5_qp"), ("k", "h_Wk", "s5_kp"),
                             ("v", "h_Wv1", "s5_vp")):
        wt = load_w_lhsT(p, g[wname], 0, 8, 1024)
        tiles = [(0, HW)] + [(HW + i * TT, TT) for i in range(ntt)]
        for o0, wd in tiles:
            xt = SH[:, 8192 + ((o0 // TT) % 2) * 4096:
                    8192 + ((o0 // TT) % 2) * 4096 + 8 * wd] \
                .rearrange("p (c t) -> p c t", c=8)
            p.dma("qS", xt, g["s5_xE"].rearrange("(c p) t -> p c t", p=128)
                  [:, :, o0:o0 + wd])
            ot = SH[:, 16384 + (((o0 // TT) % 2)) * 4096:
                    16384 + (((o0 // TT) % 2)) * 4096 + 8 * wd] \
                .rearrange("p (c t) -> p c t", c=8)
            for oc in range(8):
                ps = g["ps%d" % (oc % 4)][:, 0:wd]
                p.mm(ps, [(wt[:, kc, oc * 128:(oc + 1) * 128], xt[:, kc, :])
                          for kc in range(8)])
                p.op("scalar", "copy", out=ot[:, oc, :], in_=ps,
                     reads=[ps], writes=[ot[:, oc, :]])
            p.dma("qG", g[dstp].rearrange("(c p) t -> p c t", p=128)
                  [:, :, o0:o0 + wd], ot)

    if S5_CUT < 2:
        return
    wbf = SA[:, 0:32].rearrange("p (kc m) -> p kc m", kc=8)
    p.dma("qS", wbf, g["l1a_Wb"].rearrange("(kc p) m -> p kc m", p=128))
    wb = SH[:, 0:8 * 4].rearrange("p (kc m) -> p kc m", kc=8)
    for kc in range(8):
        p.op("scalar", "copy", out=wb[:, kc, :], in_=wbf[:, kc, :],
             reads=[wbf[:, kc, :]], writes=[wb[:, kc, :]])
    for tt in range(ntt):
        xt = SH[:, 8192 + (tt % 2) * 4096:8192 + (tt % 2) * 4096 + 4096] \
            .rearrange("p (c t) -> p c t", c=8)
        p.dma("qS", xt, chv(g["s5_lnh"], tt))
        ps = g["ps4"][:, 0:TT]
        p.mm(ps[0:4, :], [(wb[:, kc, :], xt[:, kc, :]) for kc in range(8)])
        bt = SA[0:4, 0:TT]
        p.op("scalar", "activation", out=bt, in_=ps[0:4, :], func=ACT.Sigmoid,
             reads=[ps[0:4, :]], writes=[bt])
        p.dma("qG", g["s5_beta"][:, tt * TT:(tt + 1) * TT], bt)

    def conv4(srcp, wcol0, dstp, o_lo, o_hi):
        tl = []
        oo = o_lo
        while oo < o_hi:
            tl.append((oo, min(TT, o_hi - oo)))
            oo += min(TT, o_hi - oo)
        for ch in range(8):
            colv = g[srcp].rearrange("(c p) t -> p c t", p=128)[:, ch, :]
            for it, (o0, wd) in enumerate(tl):
                ib = SA[:, (it % 2) * 1024:(it % 2) * 1024 + wd + 4]
                p.dma("qS", ib, colv[:, o0 - 4:o0 + wd])
                ob = SA[:, 2048 + (it % 2) * 512:2048 + (it % 2) * 512 + wd]
                for j in range(4):
                    wc = SM[:, wcol0 + ch * 4 + j:wcol0 + ch * 4 + j + 1]
                    if j == 0:
                        p.op("vector", "tensor_scalar", out=ob,
                             in0=ib[:, j + 1:j + 1 + wd], scalar1=wc,
                             scalar2=None, op0=ALU.mult,
                             reads=[ib[:, j + 1:j + 1 + wd], wc], writes=[ob])
                    else:
                        p.op("vector", "scalar_tensor_tensor", out=ob,
                             in0=ib[:, j + 1:j + 1 + wd], scalar=wc, in1=ob,
                             op0=ALU.mult, op1=ALU.add,
                             reads=[ib[:, j + 1:j + 1 + wd], wc, ob],
                             writes=[ob])
                p.op("scalar", "activation", out=ob, in_=ob, func=ACT.Silu,
                     reads=[ob], writes=[ob])
                p.dma("qG", g[dstp].rearrange("(c p) t -> p c t", p=128)
                      [:, ch, o0 - o_lo:o0 - o_lo + wd], ob)

    if S5_CUT < 3:
        return
    conv4("s5_qp", 204, "s5_qd", HW, EXT)
    conv4("s5_kp", 236, "s5_kd", HW, EXT)
    conv4("s5_vp", 268, "s5_vd", HW - 32, EXT)

    if S5_CUT < 4:
        return
    for tt in range(ntt):
        par = tt % 2
        qd = SA[:, par * 8192:par * 8192 + 4096].rearrange(
            "p (c t) -> p c t", c=8)
        kd = SA[:, par * 8192 + 4096:par * 8192 + 8192].rearrange(
            "p (c t) -> p c t", c=8)
        vd = SA[:, 16384 + par * 4096:16384 + par * 4096 + 4096].rearrange(
            "p (c t) -> p c t", c=8)
        p.dma("qS", qd, chv(g["s5_qd"], tt))
        p.dma("qS", kd, chv(g["s5_kd"], tt))
        p.dma("qS", vd, g["s5_vd"].rearrange("(c p) t -> p c t", p=128)
              [:, :, 32 + tt * TT:32 + (tt + 1) * TT])
        brow = SA[0:1, 24576:24576 + 4 * TT].rearrange(
            "p (h t) -> p h t", h=4)
        for hh in range(4):
            p.dma("qS", brow[:, hh, :],
                  g["s5_beta"][hh:hh + 1, tt * TT:(tt + 1) * TT])
        sq = SA[:, 25600:25600 + 512]
        for hh in range(4):
            for which, dat in (("q", qd), ("k", kd)):
                pr = g["ps%d" % (4 + (0 if which == "q" else 1))][0:1, 0:TT]
                for c2 in range(2):
                    ch = 2 * hh + c2
                    p.op("scalar", "activation", out=sq, in_=dat[:, ch, :],
                         func=ACT.Square, reads=[dat[:, ch, :]], writes=[sq])
                    p.mm(pr, [(CF[:, 0:1], sq)], cont=(c2 == 1),
                         stop=(c2 == 1))
                rr = ST[0:1, (0 if which == "q" else 1) * TT:
                        (1 if which == "q" else 2) * TT]
                p.op("scalar", "activation", out=rr, in_=pr, func=ACT.Sqrt,
                     bias=p.cst(1e-6, 1), reads=[pr], writes=[rr])
                p.op("vector", "reciprocal", out=rr, in_=rr, reads=[rr],
                     writes=[rr])
                bcast(p, g["ps6"], rr)
                for c2 in range(2):
                    ch = 2 * hh + c2
                    p.op("vector", "tensor_tensor", out=dat[:, ch, :],
                         in0=dat[:, ch, :], in1=g["ps6"][:, 0:TT],
                         op=ALU.mult,
                         reads=[dat[:, ch, :], g["ps6"][:, 0:TT]],
                         writes=[dat[:, ch, :]])
            bcast(p, g["ps7"], brow[0:1, hh, :])
            for c2 in range(2):
                ch = 2 * hh + c2
                kbv = SA[:, 26624 + c2 * 512:26624 + c2 * 512 + 512]
                p.op("vector", "tensor_tensor", out=kbv, in0=kd[:, ch, :],
                     in1=g["ps7"][:, 0:TT], op=ALU.mult,
                     reads=[kd[:, ch, :], g["ps7"][:, 0:TT]], writes=[kbv])
                p.dma("qG", g["s5_kb"].rearrange("(c p) t -> p c t", p=128)
                      [:, ch, tt * TT:(tt + 1) * TT], kbv)
                vbv = SA[:, 25600 + 512 + c2 * 512:25600 + 512 + c2 * 512 + 512]
                p.op("vector", "tensor_tensor", out=vbv, in0=vd[:, ch, :],
                     in1=g["ps7"][:, 0:TT], op=ALU.mult,
                     reads=[vd[:, ch, :], g["ps7"][:, 0:TT]], writes=[vbv])
                p.dma("qG", g["s5_vb"].rearrange("(c p) t -> p c t", p=128)
                      [:, ch, tt * TT:(tt + 1) * TT], vbv)
        p.dma("qG", chv(g["s5_qn"], tt), qd)
        p.dma("qG", chv(g["s5_kn"], tt), kd)

    if S5_CUT < 5:
        return
    nck = p.tpc // 128
    one128 = SA[:, 27008:27136]
    p.op("vector", "memset", one128, 1.0, writes=[one128])
    msl = SA[:, 27136:27264]
    msu = SA[:, 27264:27392]
    msug = SA[:, 27392:27520]
    p.op("gpsimd", "affine_select", out=msl, in_=one128, pattern=[[-1, 128]],
         compare_op=ALU.is_ge, fill=0.0, base=-1, channel_multiplier=1,
         reads=[one128], writes=[msl])
    p.op("gpsimd", "affine_select", out=msu, in_=one128, pattern=[[1, 128]],
         compare_op=ALU.is_ge, fill=0.0, base=0, channel_multiplier=-1,
         reads=[one128], writes=[msu])
    p.op("gpsimd", "affine_select", out=msug, in_=one128, pattern=[[1, 128]],
         compare_op=ALU.is_ge, fill=0.0, base=-1, channel_multiplier=-1,
         reads=[one128], writes=[msug])
    for hh in range(4):
        Sv = SA[:, 20480 + hh * 1024:20480 + hh * 1024 + 1024].rearrange(
            "p (c t) -> p c t", c=2)
        p.op("vector", "memset", Sv[:, :, :], 0.0, writes=[Sv[:, :, :]])
        for c2 in range(2):
            p.op("vector", "tensor_copy",
                 out=Sv[:, c2, 256 + c2 * 128:384 + c2 * 128], in_=CF[:, 132:260],
                 reads=[CF[:, 132:260]], writes=[Sv[:, c2, 256 + c2 * 128:384 + c2 * 128]])
    SCR = 0
    for ck in range(nck):
        for hh in range(4):
            Sv = SA[:, 20480 + hh * 1024:20480 + hh * 1024 + 1024].rearrange(
                "p (c t) -> p c t", c=2)
            qc = SA[:, SCR + 0:SCR + 256].rearrange("p (c t) -> p c t", c=2)
            kc_ = SA[:, SCR + 256:SCR + 512].rearrange("p (c t) -> p c t", c=2)
            bc_ = SA[:, SCR + 512:SCR + 768].rearrange("p (c t) -> p c t", c=2)
            vc = SA[:, SCR + 768:SCR + 1024].rearrange("p (c t) -> p c t", c=2)
            for nm, buf in (("s5_qn", qc), ("s5_kn", kc_), ("s5_kb", bc_),
                            ("s5_vb", vc)):
                p.dma("qS", buf, g[nm].rearrange("(c p) t -> p c t", p=128)
                      [:, 2 * hh:2 * hh + 2, ck * 128:(ck + 1) * 128])
            p.mm(g["ps0"][:, 0:128], [(bc_[:, c2, :], kc_[:, c2, :])
                                      for c2 in range(2)])
            p.mm(g["ps1"][:, 0:128], [(kc_[:, c2, :], bc_[:, c2, :])
                                      for c2 in range(2)])
            p.mm(g["ps2"][:, 0:128], [(kc_[:, c2, :], qc[:, c2, :])
                                      for c2 in range(2)])
            W = SA[:, SCR + 1024:SCR + 1152]
            WT = SA[:, SCR + 1152:SCR + 1280]
            VT = SA[:, SCR + 1280:SCR + 1408]
            atT = SA[:, SCR + 1408:SCR + 1536]
            p.op("vector", "tensor_tensor", out=W, in0=g["ps0"][:, 0:128],
                 in1=msl, op=ALU.mult, reads=[g["ps0"][:, 0:128], msl],
                 writes=[W])
            p.op("vector", "tensor_tensor", out=WT, in0=g["ps1"][:, 0:128],
                 in1=msug, op=ALU.mult, reads=[g["ps1"][:, 0:128], msug],
                 writes=[WT])
            p.op("vector", "tensor_tensor", out=atT, in0=g["ps2"][:, 0:128],
                 in1=msu, op=ALU.mult, reads=[g["ps2"][:, 0:128], msu],
                 writes=[atT])
            p.op("vector", "tensor_tensor", out=VT, in0=CF[:, 132:260],
                 in1=WT, op=ALU.subtract, reads=[CF[:, 132:260], WT],
                 writes=[VT])
            t1 = SA[:, SCR + 1536:SCR + 1664]
            t2 = SA[:, SCR + 1664:SCR + 1792]
            for lvl in range(6):
                p.mm(g["ps0"][:, 0:128], [(WT, W)])
                p.mm(g["ps1"][:, 0:128], [(W, WT)])
                p.op("vector", "tensor_copy", out=t1, in_=g["ps0"][:, 0:128],
                     reads=[g["ps0"][:, 0:128]], writes=[t1])
                p.op("vector", "tensor_copy", out=t2, in_=g["ps1"][:, 0:128],
                     reads=[g["ps1"][:, 0:128]], writes=[t2])
                p.mm(g["ps2"][:, 0:128], [(t1, VT)])
                p.op("vector", "tensor_copy", out=W, in_=t1, reads=[t1],
                     writes=[W])
                p.op("vector", "tensor_copy", out=WT, in_=t2, reads=[t2],
                     writes=[WT])
                p.op("vector", "tensor_add", out=VT, in0=VT,
                     in1=g["ps2"][:, 0:128], reads=[VT, g["ps2"][:, 0:128]],
                     writes=[VT])
            p.mm(g["ps3"][:, 0:512], [(bc_[:, c2, :], Sv[:, c2, :])
                                      for c2 in range(2)])
            vtm = SA[:, SCR + 1792:SCR + 2048]
            for c2 in range(2):
                p.op("tensor", "transpose", out=g["ps4"][:, 0:128],
                     in_=vc[:, c2, :], identity=IDY,
                     reads=[vc[:, c2, :], IDY], writes=[g["ps4"][:, 0:128]])
                p.op("vector", "tensor_copy",
                     out=vtm[:, c2 * 128:(c2 + 1) * 128],
                     in_=g["ps4"][:, 0:128], reads=[g["ps4"][:, 0:128]],
                     writes=[vtm[:, c2 * 128:(c2 + 1) * 128]])
            D = SA[:, SCR + 2048:SCR + 2560]
            p.op("vector", "tensor_sub", out=D[:, 0:256], in0=vtm,
                 in1=g["ps3"][:, 0:256], reads=[vtm, g["ps3"][:, 0:256]],
                 writes=[D[:, 0:256]])
            p.op("vector", "tensor_scalar", out=D[:, 256:512],
                 in0=g["ps3"][:, 256:512], scalar1=-1.0, scalar2=None,
                 op0=ALU.mult, reads=[g["ps3"][:, 256:512]],
                 writes=[D[:, 256:512]])
            p.mm(g["ps5"][:, 0:512], [(VT, D)])
            ua = SA[:, SCR + 2560:SCR + 3072]
            p.op("vector", "tensor_copy", out=ua, in_=g["ps5"][:, 0:512],
                 reads=[g["ps5"][:, 0:512]], writes=[ua])
            p.mm(g["ps6"][:, 0:512], [(qc[:, c2, :], Sv[:, c2, :])
                                      for c2 in range(2)], stop=False)
            p.mm(g["ps6"][:, 0:512], [(atT, ua)], cont=True)
            ktm = SA[:, SCR + 3072:SCR + 3328]
            for c2 in range(2):
                p.op("tensor", "transpose", out=g["ps4"][:, 128:256],
                     in_=kc_[:, c2, :], identity=IDY,
                     reads=[kc_[:, c2, :], IDY], writes=[g["ps4"][:, 128:256]])
                p.op("vector", "tensor_copy",
                     out=ktm[:, c2 * 128:(c2 + 1) * 128],
                     in_=g["ps4"][:, 128:256], reads=[g["ps4"][:, 128:256]],
                     writes=[ktm[:, c2 * 128:(c2 + 1) * 128]])
            for c2 in range(2):
                p.mm(g["ps7"][:, 0:512], [(ktm[:, c2 * 128:(c2 + 1) * 128], ua)])
                p.op("vector", "tensor_add", out=Sv[:, c2, :], in0=Sv[:, c2, :],
                     in1=g["ps7"][:, 0:512],
                     reads=[Sv[:, c2, :], g["ps7"][:, 0:512]],
                     writes=[Sv[:, c2, :]])
            ob = SA[:, SCR + 3328:SCR + 3840]
            p.op("vector", "tensor_copy", out=ob, in_=g["ps6"][:, 0:512],
                 reads=[g["ps6"][:, 0:512]], writes=[ob])
            for c2 in range(4):
                p.op("tensor", "transpose", out=g["ps4"][:, 256:384],
                     in_=ob[:, c2 * 128:(c2 + 1) * 128], identity=IDY,
                     reads=[ob[:, c2 * 128:(c2 + 1) * 128], IDY],
                     writes=[g["ps4"][:, 256:384]])
                tb = SA[:, SCR + 3840 + (c2 % 2) * 128:
                        SCR + 3840 + (c2 % 2) * 128 + 128]
                p.op("vector", "tensor_copy", out=tb, in_=g["ps4"][:, 256:384],
                     reads=[g["ps4"][:, 256:384]], writes=[tb])
                dtensor = "s5_d0" if c2 < 2 else "s5_GT"
                dch = 2 * hh + (c2 % 2)
                p.dma("qG", g[dtensor].rearrange("(c p) t -> p c t", p=128)
                      [:, dch, ck * 128:(ck + 1) * 128], tb)
    if S5_CUT < 6:
        return
    for hh in range(4):
        p.dma("qG", g["s5_sfin"][hh].rearrange("(c p) t -> p c t", p=128),
              SA[:, 20480 + hh * 1024:20480 + hh * 1024 + 1024]
              .rearrange("p (c t) -> p c t", c=2))
    p.coll("AllGather", ALU.bypass, GRPS, [g["s5_sfin"]], [g["s5_sfout"]])
    for hh in range(4):
        Tcur = SA[:, 0:512].rearrange("p (c t) -> p c t", c=2)
        sin = SA[:, 512:1024].rearrange("p (c t) -> p c t", c=2)
        p.op("vector", "memset", sin[:, :, :], 0.0, writes=[sin[:, :, :]])
        for j in range(3):
            S0j = SA[:, 1024:1536].rearrange("p (c t) -> p c t", c=2)
            Pj = SA[:, 1536:2048].rearrange("p (c t) -> p c t", c=2)
            p.dma("qS", S0j, g["s5_sfout"][j, hh, :, 0:256]
                  .rearrange("(c p) t -> p c t", p=128))
            p.dma("qS", Pj, g["s5_sfout"][j, hh, :, 256:512]
                  .rearrange("(c p) t -> p c t", p=128))
            if j == 0:
                for c2 in range(2):
                    p.op("vector", "tensor_copy", out=Tcur[:, c2, :],
                         in_=S0j[:, c2, :], reads=[S0j[:, c2, :]],
                         writes=[Tcur[:, c2, :]])
            else:
                PjT = SA[:, 2048:2560].rearrange("p (c t) -> p c t", c=2)
                for c2 in range(2):
                    for d2 in range(2):
                        p.op("tensor", "transpose", out=g["ps4"][:, 0:128],
                             in_=Pj[:, c2, d2 * 128:(d2 + 1) * 128],
                             identity=IDY,
                             reads=[Pj[:, c2, d2 * 128:(d2 + 1) * 128], IDY],
                             writes=[g["ps4"][:, 0:128]])
                        p.op("vector", "tensor_copy",
                             out=PjT[:, d2, c2 * 128:(c2 + 1) * 128],
                             in_=g["ps4"][:, 0:128],
                             reads=[g["ps4"][:, 0:128]],
                             writes=[PjT[:, d2, c2 * 128:(c2 + 1) * 128]])
                for c2 in range(2):
                    p.mm(g["ps5"][:, 0:256],
                         [(PjT[:, d2, c2 * 128:(c2 + 1) * 128], Tcur[:, d2, :])
                          for d2 in range(2)])
                    p.op("vector", "tensor_add", out=Tcur[:, c2, :],
                         in0=S0j[:, c2, :], in1=g["ps5"][:, 0:256],
                         reads=[S0j[:, c2, :], g["ps5"][:, 0:256]],
                         writes=[Tcur[:, c2, :]])
            sel = SM[:, 164 + 1 + j:165 + 1 + j]
            for c2 in range(2):
                p.op("vector", "scalar_tensor_tensor", out=sin[:, c2, :],
                     in0=Tcur[:, c2, :], scalar=sel, in1=sin[:, c2, :],
                     op0=ALU.mult, op1=ALU.add,
                     reads=[Tcur[:, c2, :], sel, sin[:, c2, :]],
                     writes=[sin[:, c2, :]])
        for tt in range(ntt):
            gt = SA[:, 1024 + (tt % 2) * 1024:1024 + (tt % 2) * 1024 + 1024] \
                .rearrange("p (c t) -> p c t", c=2)
            p.dma("qS", gt, g["s5_GT"].rearrange("(c p) t -> p c t", p=128)
                  [:, 2 * hh:2 * hh + 2, tt * TT:(tt + 1) * TT])
            d0 = SA[:, 3072 + (tt % 2) * 1024:3072 + (tt % 2) * 1024 + 1024] \
                .rearrange("p (c t) -> p c t", c=2)
            p.dma("qS", d0, g["s5_d0"].rearrange("(c p) t -> p c t", p=128)
                  [:, 2 * hh:2 * hh + 2, tt * TT:(tt + 1) * TT])
            dh = SH[:, (tt % 2) * 1024:(tt % 2) * 1024 + 1024].rearrange(
                "p (c t) -> p c t", c=2)
            for c2 in range(2):
                p.mm(g["ps%d" % c2][:, 0:TT],
                     [(sin[:, d2, c2 * 128:(c2 + 1) * 128], gt[:, d2, :])
                      for d2 in range(2)])
                p.op("vector", "tensor_add", out=d0[:, c2, :],
                     in0=d0[:, c2, :], in1=g["ps%d" % c2][:, 0:TT],
                     reads=[d0[:, c2, :], g["ps%d" % c2][:, 0:TT]],
                     writes=[d0[:, c2, :]])
                p.op("scalar", "copy", out=dh[:, c2, :], in_=d0[:, c2, :],
                     reads=[d0[:, c2, :]], writes=[dh[:, c2, :]])
            p.dma("qG", g["s5_delta"].rearrange("(c p) t -> p c t", p=128)
                  [:, 2 * hh:2 * hh + 2, tt * TT:(tt + 1) * TT], dh)

    if S5_CUT < 7:
        return
    for nm, wc0, K, dstp in (("f3", 304, 3, "s5_f3"), ("f7", 328, 7, "s5_f7"),
                             ("f15", 384, 15, "s5_f15"),
                             ("f31", 504, 31, "s5_f31"))[:S5_FIRN]:
        for ch in range(8):
            srcv = "s5_qd" if S5_FIRV == 4 else "s5_vd"
            colv = g[srcv].rearrange("(c p) t -> p c t", p=128)[:, ch, :]
            for it in range(ntt):
                o0 = 32 + it * TT
                ib = SA[:, (it % 2) * 1024:(it % 2) * 1024 + TT + 32]
                if S5_FIRV == 4:
                    p.dma("qS", ib[:, 0:TT], colv[:, it * TT:(it + 1) * TT])
                else:
                    p.dma("qS", ib, colv[:, o0 - 32:o0 + TT])
                ob = SA[:, 2048 + (it % 2) * 512:2048 + (it % 2) * 512 + TT]
                obh = SH[:, (it % 2) * 512:(it % 2) * 512 + TT]
                for j in range(K if S5_FIRV < 3 else 1):
                    wc = SM[:, wc0 + ch * K + j:wc0 + ch * K + j + 1]
                    off = 32 - (K - 1) + j
                    if S5_FIRV >= 3:
                        p.op("vector", "tensor_copy", out=ob,
                             in_=ib[:, off:off + TT],
                             reads=[ib[:, off:off + TT]], writes=[ob])
                    elif j == 0:
                        p.op("vector", "tensor_scalar", out=ob,
                             in0=ib[:, off:off + TT], scalar1=wc,
                             scalar2=None, op0=ALU.mult,
                             reads=[ib[:, off:off + TT], wc], writes=[ob])
                    else:
                        p.op("vector", "scalar_tensor_tensor", out=ob,
                             in0=ib[:, off:off + TT], scalar=wc, in1=ob,
                             op0=ALU.mult, op1=ALU.add,
                             reads=[ib[:, off:off + TT], wc, ob], writes=[ob])
                if S5_FIRV == 0:
                    p.op("scalar", "copy", out=obh, in_=ob, reads=[ob],
                         writes=[obh])
                    p.dma("qG", g[dstp].rearrange("(c p) t -> p c t", p=128)
                          [:, ch, it * TT:(it + 1) * TT], obh)
                elif S5_FIRV == 1:
                    p.dma("qG", g["s5_d0"].rearrange("(c p) t -> p c t", p=128)
                          [:, ch, it * TT:(it + 1) * TT], ob)
                else:
                    pass

    if S5_CUT < 8:
        return
    rparts = ("s5_lnh", "s5_f3", "s5_f7", "s5_f15", "s5_f31", "s5_delta")
    for tt in range(ntt):
        rin = SH[:, 8192:8192 + 48 * 512].rearrange("p (c t) -> p c t", c=48)
        for pi, nm in enumerate(rparts):
            p.dma("qS", rin[:, pi * 8:(pi + 1) * 8, :], chv(g[nm], tt))
        if S5_RC < 2:
            continue
        hm = SHB[:, 0:512]
        for oc in range(16):
            wt = load_w_lhsT(p, g["h_rW1"][:, oc * 128:(oc + 1) * 128],
                             0, 48, 128)
            ps = g["ps%d" % (oc % 4)][:, 0:TT]
            p.mm(ps, [(wt[:, kc, :], rin[:, kc, :]) for kc in range(48)])
            p.op("scalar", "activation", out=hm, in_=ps, func=ACT.Gelu,
                 bias=SM[:, 754 + oc:755 + oc],
                 reads=[ps, SM[:, 754 + oc:755 + oc]], writes=[hm])
            p.dma("qG", g["s5_hmid"][oc * 128:(oc + 1) * 128,
                                     tt * TT:(tt + 1) * TT], hm)
    if S5_RC < 3:
        return
    w2t = load_w_lhsT(p, g["h_rW2"], 0, 16, 128)
    ltau = SA[0:1, 0:4]
    p.dma("qS", ltau, g["l1a_ltau"])
    p.op("scalar", "activation", out=ltau, in_=ltau, func=ACT.Exp,
         reads=[ltau], writes=[ltau])
    p.op("vector", "tensor_scalar", out=ltau, in0=ltau, scalar1=0.5,
         scalar2=None, op0=ALU.max, reads=[ltau], writes=[ltau])
    p.op("vector", "reciprocal", out=ltau, in_=ltau, reads=[ltau],
         writes=[ltau])
    rt20 = SA[0:1, 16:36]
    for i5 in range(5):
        p.op("vector", "tensor_copy", out=rt20[:, i5::5], in_=ltau,
             reads=[ltau], writes=[rt20[:, i5::5]])
    bcast(p, g["ps6"][:, 0:20], rt20)
    RT20 = SA[:, 64:84]
    p.op("vector", "tensor_copy", out=RT20, in_=g["ps6"][:, 0:20],
         reads=[g["ps6"][:, 0:20]], writes=[RT20])
    for tt in range(ntt):
        hm = SH[:, 8192 + (tt % 2) * 8192:8192 + (tt % 2) * 8192 + 16 * 512] \
            .rearrange("p (c t) -> p c t", c=16)
        p.dma("qS", hm, g["s5_hmid"].rearrange("(c p) t -> p c t", p=128)
              [:, :, tt * TT:(tt + 1) * TT])
        ps = g["ps5"][:, 0:TT]
        p.mm(ps, [(w2t[:, kc, :], hm[:, kc, :]) for kc in range(16)])
        lg = SA[:, 1024:1024 + 512]
        p.op("vector", "tensor_scalar", out=lg[0:20, :], in0=ps[0:20, :],
             scalar1=SM[0:20, 753:754], scalar2=None, op0=ALU.add,
             reads=[ps[0:20, :], SM[0:20, 753:754]], writes=[lg[0:20, :]])
        if S5_RC < 4:
            continue
        for st4 in range(4):
            z = SA[:, 1536:1556]
            if S5_RC >= 5:
                p.op("tensor", "transpose", out=g["ps6"][:, 0:20],
                     in_=lg[0:20, st4 * 128:(st4 + 1) * 128],
                     identity=CF[0:20, 132:152],
                     reads=[lg[0:20, st4 * 128:(st4 + 1) * 128],
                            CF[0:20, 132:152]], writes=[g["ps6"][:, 0:20]])
                p.op("vector", "tensor_tensor", out=z, in0=g["ps6"][:, 0:20],
                     in1=RT20, op=ALU.mult, reads=[g["ps6"][:, 0:20], RT20],
                     writes=[z])
            else:
                p.op("vector", "memset", z, 0.5, writes=[z])
            zm = SA[:, 1556:1560]
            zme = SA[:, 1560:1580]
            zv = z.rearrange("p (h i) -> p h i", h=4)

            def expand5():
                for i5 in range(5):
                    p.op("vector", "tensor_copy", out=zme[:, i5::5], in_=zm,
                         reads=[zm], writes=[zme[:, i5::5]])

            p.op("vector", "tensor_reduce", out=zm, in_=zv, axis=AXX,
                 op=ALU.max, reads=[z], writes=[zm])
            expand5()
            p.op("vector", "tensor_sub", out=z, in0=z, in1=zme,
                 reads=[z, zme], writes=[z])
            p.op("scalar", "activation", out=z, in_=z, func=ACT.Exp,
                 reads=[z], writes=[z])
            p.op("vector", "tensor_reduce", out=zm, in_=zv, axis=AXX,
                 op=ALU.add, reads=[z], writes=[zm])
            p.op("vector", "reciprocal", out=zm, in_=zm, reads=[zm],
                 writes=[zm])
            expand5()
            p.op("vector", "tensor_tensor", out=z, in0=z, in1=zme,
                 op=ALU.mult, reads=[z, zme], writes=[z])
            if S5_RC >= 6:
                p.op("tensor", "transpose", out=g["ps7"][0:20, 0:128],
                     in_=z, identity=IDY, reads=[z, IDY],
                     writes=[g["ps7"][0:20, 0:128]])
                pr = SA[0:20, 2048 + st4 * 128:2048 + (st4 + 1) * 128]
                p.op("vector", "tensor_copy", out=pr,
                     in_=g["ps7"][0:20, 0:128],
                     reads=[g["ps7"][0:20, 0:128]], writes=[pr])
        if S5_RC >= 6:
            p.dma("qG", g["s5_probs"][:, tt * TT:(tt + 1) * TT],
                  SA[0:20, 2048:2560])

    if S5_CUT < 9:
        return
    wif = SA[:, 2048:2080].rearrange("p (kc m) -> p kc m", kc=8)
    p.dma("qS", wif, g["l1a_idW"].rearrange("(kc p) m -> p kc m", p=128))
    wi = SH[:, 0:32].rearrange("p (kc m) -> p kc m", kc=8)
    for kc in range(8):
        p.op("scalar", "copy", out=wi[:, kc, :], in_=wif[:, kc, :],
             reads=[wif[:, kc, :]], writes=[wi[:, kc, :]])
    for tt in range(ntt):
        xt = SH[:, 8192 + (tt % 2) * 4096:8192 + (tt % 2) * 4096 + 4096] \
            .rearrange("p (c t) -> p c t", c=8)
        p.dma("qS", xt, chv(g["s5_lnh"], tt))
        ps = g["ps4"][:, 0:TT]
        p.mm(ps[0:4, :], [(wi[:, kc, :], xt[:, kc, :]) for kc in range(8)])
        pv = SA[0:4, 0:TT]
        p.op("scalar", "activation", out=pv, in_=ps[0:4, :], func=ACT.Sigmoid,
             bias=SM[0:4, 752:753], reads=[ps[0:4, :], SM[0:4, 752:753]],
             writes=[pv])
        t_ = SA[0:4, 512:512 + TT]
        p.op("vector", "tensor_scalar", out=t_, in0=pv, scalar1=0.09,
             scalar2=0.9, op0=ALU.mult, op1=ALU.add, reads=[pv], writes=[t_])
        p.op("vector", "tensor_tensor", out=t_, in0=t_, in1=pv, op=ALU.mult,
             reads=[t_, pv], writes=[t_])
        cm_ = SA[0:4, 1024:1024 + TT]
        p.op("vector", "tensor_scalar", out=cm_, in0=t_, scalar1=-1.0,
             scalar2=1.0, op0=ALU.mult, op1=ALU.add, reads=[t_], writes=[cm_])
        p.dma("qG", g["s5_gate"][0:4, tt * TT:(tt + 1) * TT], cm_)
        p.dma("qG", g["s5_gate"][4:8, tt * TT:(tt + 1) * TT], t_)

    if S5_CUT < 10:
        return
    paths5 = ("s5_f3", "s5_f7", "s5_f15", "s5_f31", "s5_delta")
    for tt in range(ntt):
        vd = SA[:, 8192:8192 + 4096].rearrange("p (c t) -> p c t", c=8)
        p.dma("qS", vd, g["s5_vd"].rearrange("(c p) t -> p c t", p=128)
              [:, :, 32 + tt * TT:32 + (tt + 1) * TT])
        on = SH[:, 0:4096].rearrange("p (c t) -> p c t", c=8)
        for hh in range(4):
            ctx = SA[:, 12288 + 0:12288 + 1024].rearrange(
                "p (c t) -> p c t", c=2)
            for pi, nm in enumerate(paths5):
                pb = SH[:, 4096 + (pi % 2) * 1024:4096 + (pi % 2) * 1024 + 1024] \
                    .rearrange("p (c t) -> p c t", c=2)
                p.dma("qS", pb, g[nm].rearrange("(c p) t -> p c t", p=128)
                      [:, 2 * hh:2 * hh + 2, tt * TT:(tt + 1) * TT])
                prow = SA[0:1, 13312:13312 + TT]
                p.dma("qS", prow,
                      g["s5_probs"][5 * hh + pi:5 * hh + pi + 1,
                                    tt * TT:(tt + 1) * TT])
                bcast(p, g["ps6"], prow)
                for c2 in range(2):
                    if pi == 0:
                        p.op("vector", "tensor_tensor", out=ctx[:, c2, :],
                             in0=pb[:, c2, :], in1=g["ps6"][:, 0:TT],
                             op=ALU.mult,
                             reads=[pb[:, c2, :], g["ps6"][:, 0:TT]],
                             writes=[ctx[:, c2, :]])
                    else:
                        tb = SA[:, 14336:14336 + TT]
                        p.op("vector", "tensor_tensor", out=tb,
                             in0=pb[:, c2, :], in1=g["ps6"][:, 0:TT],
                             op=ALU.mult,
                             reads=[pb[:, c2, :], g["ps6"][:, 0:TT]],
                             writes=[tb])
                        p.op("vector", "tensor_add", out=ctx[:, c2, :],
                             in0=ctx[:, c2, :], in1=tb,
                             reads=[ctx[:, c2, :], tb],
                             writes=[ctx[:, c2, :]])
            crow = SA[0:1, 13312 + 512:13312 + 512 + TT]
            p.dma("qS", crow, g["s5_gate"][hh:hh + 1, tt * TT:(tt + 1) * TT])
            bcast(p, g["ps6"], crow)
            arow = SA[0:1, 14848:14848 + TT]
            p.dma("qS", arow, g["s5_gate"][4 + hh:5 + hh,
                                           tt * TT:(tt + 1) * TT])
            bcast(p, g["ps7"], arow)
            for c2 in range(2):
                ch = 2 * hh + c2
                p.op("vector", "tensor_tensor", out=ctx[:, c2, :],
                     in0=ctx[:, c2, :], in1=g["ps6"][:, 0:TT], op=ALU.mult,
                     reads=[ctx[:, c2, :], g["ps6"][:, 0:TT]],
                     writes=[ctx[:, c2, :]])
                tb = SA[:, 14336:14336 + TT]
                p.op("vector", "tensor_tensor", out=tb, in0=vd[:, ch, :],
                     in1=g["ps7"][:, 0:TT], op=ALU.mult,
                     reads=[vd[:, ch, :], g["ps7"][:, 0:TT]], writes=[tb])
                p.op("vector", "tensor_add", out=ctx[:, c2, :],
                     in0=ctx[:, c2, :], in1=tb,
                     reads=[ctx[:, c2, :], tb], writes=[ctx[:, c2, :]])
            sq = SA[:, 14336:14336 + TT]
            pr = g["ps5"][0:1, 0:TT]
            for c2 in range(2):
                p.op("scalar", "activation", out=sq, in_=ctx[:, c2, :],
                     func=ACT.Square, reads=[ctx[:, c2, :]], writes=[sq])
                p.mm(pr, [(CF[:, 0:1], sq)], cont=(c2 == 1), stop=(c2 == 1))
            rr = ST[0:1, 0:TT]
            p.op("vector", "tensor_scalar", out=rr, in0=pr, scalar1=1.0 / 256.0,
                 scalar2=None, op0=ALU.mult, reads=[pr], writes=[rr])
            p.op("scalar", "activation", out=rr, in_=rr, func=ACT.Sqrt,
                 bias=p.cst(1e-5, 1), reads=[rr], writes=[rr])
            p.op("vector", "reciprocal", out=rr, in_=rr, reads=[rr],
                 writes=[rr])
            bcast(p, g["ps6"], rr)
            for c2 in range(2):
                ch = 2 * hh + c2
                p.op("vector", "tensor_tensor", out=ctx[:, c2, :],
                     in0=ctx[:, c2, :], in1=g["ps6"][:, 0:TT], op=ALU.mult,
                     reads=[ctx[:, c2, :], g["ps6"][:, 0:TT]],
                     writes=[ctx[:, c2, :]])
                p.op("vector", "tensor_scalar", out=on[:, ch, :],
                     in0=ctx[:, c2, :], scalar1=SM[:, 300 + c2:301 + c2],
                     scalar2=None, op0=ALU.mult,
                     reads=[ctx[:, c2, :], SM[:, 300 + c2:301 + c2]],
                     writes=[on[:, ch, :]])
        p.dma("qG", chv(g["s5_on"], tt), on)
    wo = load_w_lhsT(p, g["h_Wo"], 8192, 8, 1024)
    for tt in range(ntt):
        ov = SH[:, 0 + (tt % 2) * 4096:(tt % 2) * 4096 + 4096].rearrange(
            "p (c t) -> p c t", c=8)
        p.dma("qS", ov, chv(g["s5_on"], tt))
        xv = SA[:, (tt % 2) * 8192:(tt % 2) * 8192 + 4096].rearrange(
            "p (c t) -> p c t", c=8)
        p.dma("qS", xv, chv(g[src], tt))
        yv = SA[:, (tt % 2) * 8192 + 4096:(tt % 2) * 8192 + 8192].rearrange(
            "p (c t) -> p c t", c=8)
        for oc in range(8):
            ps = g["ps%d" % (oc % 4)][:, 0:TT]
            p.mm(ps, [(wo[:, kc, oc * 128:(oc + 1) * 128], ov[:, kc, :])
                      for kc in range(8)])
            p.op("vector", "tensor_add", out=yv[:, oc, :], in0=xv[:, oc, :],
                 in1=ps, reads=[xv[:, oc, :], ps], writes=[yv[:, oc, :]])
        p.dma("qG", chv(g[dst], tt), yv)


def _xmax_col_to_scalar(p: P, col, tmp_ps, out_row):
    g = p.g
    p.op("tensor", "transpose", out=tmp_ps[0:1, 0:128], in_=col,
         identity=g["CF"][:, 132:260],
         reads=[col, g["CF"][:, 132:260]], writes=[tmp_ps[0:1, 0:128]])
    p.op("vector", "tensor_reduce", out=out_row, in_=tmp_ps[0:1, 0:128],
         axis=AXX, op=ALU.max, reads=[tmp_ps[0:1, 0:128]], writes=[out_row])


def emit_s6(p: P, src, dst, do_quant, call_id):
    g = p.g
    ntt = p.ntt
    SA, SH, ST, SM, CF, SHB = g["SA"], g["SH"], g["ST"], g["SM"], g["CF"], g["SHB"]
    ALL8 = [[0, 1, 2, 3, 4, 5, 6, 7]]
    lnf = g["s6_lnf"]
    emit_ln(p, g[src], [(lnf, F32)])

    if do_quant:
        for wname, wTname, wiTname, sname, rows, cols in (
                ("l1f_wg", "l1f_wgT", "h_wgT", "s6_sg", 2816, 1024),
                ("l1f_wu", "l1f_wuT", "h_wuT", "s6_su", 2816, 1024),
                ("l1f_wd", "l1f_wdT", "h_wdT", "s6_sd", 1024, 2816)):
            npt = rows // 128
            for pt in range(npt):
                wv = SA[:, (pt % 2) * 4096:(pt % 2) * 4096 + cols]
                p.dma("qS", wv, g[wname][pt * 128:(pt + 1) * 128, :])
                mx = SA[:, 8192 + (pt % 2):8193 + (pt % 2)]
                p.op("vector", "tensor_reduce", out=mx, in_=wv, axis=AXX,
                     op=ALU.max, apply_absolute_value=True,
                     reads=[wv], writes=[mx])
                p.op("vector", "tensor_scalar", out=mx, in0=mx,
                     scalar1=1.0 / 127.0, scalar2=None, op0=ALU.mult,
                     reads=[mx], writes=[mx])
                p.op("vector", "reciprocal", out=mx, in_=mx, reads=[mx],
                     writes=[mx])
                p.dma("qG", g[sname][pt * 128:(pt + 1) * 128, :], mx)
                p.op("tensor", "transpose", out=g["ps6"][0:1, 0:128], in_=mx,
                     identity=CF[:, 132:260],
                     reads=[mx, CF[:, 132:260]], writes=[g["ps6"][0:1, 0:128]])
                mrow = ST[0:1, 512:640]
                p.op("vector", "tensor_copy", out=mrow,
                     in_=g["ps6"][0:1, 0:128], reads=[g["ps6"][0:1, 0:128]],
                     writes=[mrow])
                p.dma("qG", g[sname + "r"][:, pt * 128:(pt + 1) * 128], mrow)
            nct = cols // 128
            for pt in range(nct):
                wv = SA[:, (pt % 2) * 8192:(pt % 2) * 8192 + rows]
                p.dma("qS", wv, g[wTname][pt * 128:(pt + 1) * 128, :])
                rrow = ST[0:1, 0:512]
                hv = SH[:, (pt % 2) * 4096:(pt % 2) * 4096 + rows]
                oo = 0
                while oo < rows:
                    wdt_ = min(512, rows - oo)
                    p.dma("qS", rrow[:, 0:wdt_],
                          g[sname + "r"][:, oo:oo + wdt_])
                    bcast(p, g["ps6"], rrow[:, 0:wdt_])
                    sl = wv[:, oo:oo + wdt_]
                    p.op("vector", "tensor_tensor", out=sl, in0=sl,
                         in1=g["ps6"][:, 0:wdt_], op=ALU.mult,
                         reads=[sl, g["ps6"][:, 0:wdt_]], writes=[sl])
                    p.op("vector", "tensor_scalar", out=sl, in0=sl,
                         scalar1=MAGIC, scalar2=None, op0=ALU.add,
                         reads=[sl], writes=[sl])
                    p.op("vector", "tensor_scalar", out=sl, in0=sl,
                         scalar1=MAGIC, scalar2=None, op0=ALU.subtract,
                         reads=[sl], writes=[sl])
                    p.op("vector", "tensor_scalar", out=sl, in0=sl,
                         scalar1=-128.0, scalar2=127.0, op0=ALU.max,
                         op1=ALU.min, reads=[sl], writes=[sl])
                    p.op("scalar", "copy", out=hv[:, oo:oo + wdt_],
                         in_=sl, reads=[sl], writes=[hv[:, oo:oo + wdt_]])
                    oo += wdt_
                p.dma("qG", g[wiTname][pt * 128:(pt + 1) * 128, :], hv)

    mxc = SA[:, 0:1]
    p.op("vector", "memset", mxc, 0.0, writes=[mxc])
    for tt in range(ntt):
        xv = SA[:, 1024 + (tt % 2) * 4096:1024 + (tt % 2) * 4096 + 4096]
        p.dma("qS", xv, chv(lnf, tt))
        t1 = SA[:, 16:17]
        p.op("vector", "tensor_reduce", out=t1, in_=xv, axis=AXX, op=ALU.max,
             apply_absolute_value=True, reads=[xv], writes=[t1])
        p.op("vector", "tensor_tensor", out=mxc, in0=mxc, in1=t1, op=ALU.max,
             reads=[mxc, t1], writes=[mxc])
    gmr = ST[0:1, 0:1]
    _xmax_col_to_scalar(p, mxc, g["ps6"], gmr)
    p.dma("qG", g["s6_gin"], gmr)
    p.coll("AllReduce", ALU.max, ALL8, [g["s6_gin"]], [g["s6_gout"]])
    clip = SA[0:1, 32:33]
    p.dma("qS", clip, g["l1f_clx"])
    p.op("scalar", "activation", out=clip, in_=clip, func=ACT.Sigmoid,
         reads=[clip], writes=[clip])
    gm = SA[0:1, 33:34]
    p.dma("qS", gm, g["s6_gout"])
    sxr = SA[0:1, 34:35]
    p.op("vector", "tensor_tensor", out=sxr, in0=clip, in1=gm, op=ALU.mult,
         reads=[clip, gm], writes=[sxr])
    p.op("vector", "tensor_scalar", out=sxr, in0=sxr, scalar1=1.0 / 127.0,
         scalar2=None, op0=ALU.mult, reads=[sxr], writes=[sxr])
    rsx = SA[0:1, 35:36]
    p.op("vector", "reciprocal", out=rsx, in_=sxr, reads=[sxr], writes=[rsx])
    bcast(p, g["ps6"][:, 0:1], rsx)
    RSX = SA[:, 36:37]
    p.op("vector", "tensor_copy", out=RSX, in_=g["ps6"][:, 0:1],
         reads=[g["ps6"][:, 0:1]], writes=[RSX])
    bcast(p, g["ps6"][:, 0:1], sxr)
    SXC = SA[:, 37:38]
    p.op("vector", "tensor_copy", out=SXC, in_=g["ps6"][:, 0:1],
         reads=[g["ps6"][:, 0:1]], writes=[SXC])
    for tt in range(ntt):
        xv = SA[:, 1024 + (tt % 2) * 4096:1024 + (tt % 2) * 4096 + 4096]
        p.dma("qS", xv, chv(lnf, tt))
        p.op("vector", "tensor_scalar", out=xv, in0=xv, scalar1=RSX,
             scalar2=MAGIC, op0=ALU.mult, op1=ALU.add,
             reads=[xv, RSX], writes=[xv])
        p.op("vector", "tensor_scalar", out=xv, in0=xv, scalar1=MAGIC,
             scalar2=None, op0=ALU.subtract, reads=[xv], writes=[xv])
        p.op("vector", "tensor_scalar", out=xv, in0=xv, scalar1=-128.0,
             scalar2=127.0, op0=ALU.max, op1=ALU.min, reads=[xv], writes=[xv])
        hq = SH[:, (tt % 2) * 4096:(tt % 2) * 4096 + 4096]
        p.op("scalar", "copy", out=hq, in_=xv, reads=[xv], writes=[hq])
        p.dma("qG", chv(g["s6_xq"], tt),
              hq.rearrange("p (c t) -> p c t", c=8))

    sgc = SA[:, 38:39]
    suc = SA[:, 39:40]
    mxh = SA[:, 40:41]
    p.op("vector", "memset", mxh, 0.0, writes=[mxh])
    for oc in range(22):
        wgt = load_w_lhsT(p, g["h_wgT"][:, oc * 128:(oc + 1) * 128],
                          16384, 8, 128)
        wut = load_w_lhsT(p, g["h_wuT"][:, oc * 128:(oc + 1) * 128],
                          16384 + 1024, 8, 128)
        sgr = SA[:, 42 + (oc % 2) * 2:43 + (oc % 2) * 2]
        sur = SA[:, 43 + (oc % 2) * 2:44 + (oc % 2) * 2]
        p.dma("qS", sgr, g["s6_sg"][oc * 128:(oc + 1) * 128, :])
        p.dma("qS", sur, g["s6_su"][oc * 128:(oc + 1) * 128, :])
        p.op("vector", "reciprocal", out=sgr, in_=sgr, reads=[sgr],
             writes=[sgr])
        p.op("vector", "reciprocal", out=sur, in_=sur, reads=[sur],
             writes=[sur])
        p.op("vector", "tensor_tensor", out=sgr, in0=sgr, in1=SXC,
             op=ALU.mult, reads=[sgr, SXC], writes=[sgr])
        p.op("vector", "tensor_tensor", out=sur, in0=sur, in1=SXC,
             op=ALU.mult, reads=[sur, SXC], writes=[sur])
        for tt in range(ntt):
            xq = SH[:, (tt % 2) * 4096:(tt % 2) * 4096 + 4096] \
                .rearrange("p (c t) -> p c t", c=8)
            p.dma("qS", xq, chv(g["s6_xq"], tt))
            pg = g["ps%d" % (2 * (tt % 2))][:, 0:TT]
            pu = g["ps%d" % (2 * (tt % 2) + 1)][:, 0:TT]
            p.mm(pg, [(wgt[:, kc, :], xq[:, kc, :]) for kc in range(8)])
            p.mm(pu, [(wut[:, kc, :], xq[:, kc, :]) for kc in range(8)])
            sl = SA[:, 1024 + (tt % 2) * 512:1024 + (tt % 2) * 512 + 512]
            p.op("scalar", "activation", out=sl, in_=pg, func=ACT.Silu,
                 scale=sgr, reads=[pg, sgr], writes=[sl])
            p.op("vector", "tensor_tensor", out=sl, in0=sl, in1=pu,
                 op=ALU.mult, reads=[sl, pu], writes=[sl])
            p.op("vector", "tensor_scalar", out=sl, in0=sl, scalar1=sur,
                 scalar2=None, op0=ALU.mult, reads=[sl, sur], writes=[sl])
            t1 = SA[:, 48 + (tt % 2):49 + (tt % 2)]
            p.op("vector", "tensor_reduce", out=t1, in_=sl, axis=AXX,
                 op=ALU.max, apply_absolute_value=True, reads=[sl],
                 writes=[t1])
            p.op("vector", "tensor_tensor", out=mxh, in0=mxh, in1=t1,
                 op=ALU.max, reads=[mxh, t1], writes=[mxh])
            p.dma("qG", g["s6_h"][oc * 128:(oc + 1) * 128,
                                  tt * TT:(tt + 1) * TT], sl)
    gmr2 = ST[0:1, 4:5]
    _xmax_col_to_scalar(p, mxh, g["ps6"], gmr2)
    p.dma("qG", g["s6_gin"], gmr2)
    p.coll("AllReduce", ALU.max, ALL8, [g["s6_gin"]], [g["s6_gout2"]])
    clip2 = SA[0:1, 50:51]
    p.dma("qS", clip2, g["l1f_cld"])
    p.op("scalar", "activation", out=clip2, in_=clip2, func=ACT.Sigmoid,
         reads=[clip2], writes=[clip2])
    gm2 = SA[0:1, 51:52]
    p.dma("qS", gm2, g["s6_gout2"])
    shr = SA[0:1, 52:53]
    p.op("vector", "tensor_tensor", out=shr, in0=clip2, in1=gm2, op=ALU.mult,
         reads=[clip2, gm2], writes=[shr])
    p.op("vector", "tensor_scalar", out=shr, in0=shr, scalar1=1.0 / 127.0,
         scalar2=None, op0=ALU.mult, reads=[shr], writes=[shr])
    rsh = SA[0:1, 53:54]
    p.op("vector", "reciprocal", out=rsh, in_=shr, reads=[shr], writes=[rsh])
    bcast(p, g["ps6"][:, 0:1], rsh)
    RSH = SA[:, 54:55]
    p.op("vector", "tensor_copy", out=RSH, in_=g["ps6"][:, 0:1],
         reads=[g["ps6"][:, 0:1]], writes=[RSH])
    bcast(p, g["ps6"][:, 0:1], shr)
    SHC = SA[:, 55:56]
    p.op("vector", "tensor_copy", out=SHC, in_=g["ps6"][:, 0:1],
         reads=[g["ps6"][:, 0:1]], writes=[SHC])
    for oc in range(22):
        for tt in range(ntt):
            hv = SA[:, 1024 + ((oc * ntt + tt) % 2) * 4096:
                    1024 + ((oc * ntt + tt) % 2) * 4096 + 512]
            p.dma("qS", hv, g["s6_h"][oc * 128:(oc + 1) * 128,
                                      tt * TT:(tt + 1) * TT])
            p.op("vector", "tensor_scalar", out=hv, in0=hv, scalar1=RSH,
                 scalar2=MAGIC, op0=ALU.mult, op1=ALU.add,
                 reads=[hv, RSH], writes=[hv])
            p.op("vector", "tensor_scalar", out=hv, in0=hv, scalar1=MAGIC,
                 scalar2=None, op0=ALU.subtract, reads=[hv], writes=[hv])
            p.op("vector", "tensor_scalar", out=hv, in0=hv, scalar1=-128.0,
                 scalar2=127.0, op0=ALU.max, op1=ALU.min, reads=[hv],
                 writes=[hv])
            hb = SHB[:, ((oc * ntt + tt) % 2) * 512:
                     ((oc * ntt + tt) % 2) * 512 + 512]
            p.op("scalar", "copy", out=hb, in_=hv, reads=[hv], writes=[hb])
            p.dma("qG", g["s6_hq"][oc * 128:(oc + 1) * 128,
                                   tt * TT:(tt + 1) * TT], hb)
    sdc = SA[:, 56 + 0:57]
    for och in range(8):
        wdt = load_w_lhsT(p, g["h_wdT"][:, och * 128:(och + 1) * 128],
                          0, 22, 128)
        sdr = SA[:, 58 + (och % 2):59 + (och % 2)]
        p.dma("qS", sdr, g["s6_sd"][och * 128:(och + 1) * 128, :])
        p.op("vector", "reciprocal", out=sdr, in_=sdr, reads=[sdr],
             writes=[sdr])
        p.op("vector", "tensor_tensor", out=sdr, in0=sdr, in1=SHC,
             op=ALU.mult, reads=[sdr, SHC], writes=[sdr])
        for tt in range(ntt):
            hq = SH[:, 8192 + (tt % 2) * 11264:
                    8192 + (tt % 2) * 11264 + 22 * 512] \
                .rearrange("p (kc t) -> p kc t", kc=22)
            hqs = g["s6_hq"].rearrange("(kc p) t -> p kc t", p=128)
            for k0 in range(0, 22, 8):
                k1 = min(22, k0 + 8)
                p.dma("qS", hq[:, k0:k1, :],
                      hqs[:, k0:k1, tt * TT:(tt + 1) * TT])
            ps = g["ps%d" % (4 + tt % 2)][:, 0:TT]
            p.mm(ps, [(wdt[:, kc, :], hq[:, kc, :]) for kc in range(22)])
            xv = SA[:, 1024 + (tt % 2) * 512:1024 + (tt % 2) * 512 + 512]
            p.dma("qS", xv, chv(g[src], tt)[:, och, :])
            yv = SA[:, 2048 + (tt % 2) * 512:2048 + (tt % 2) * 512 + 512]
            p.op("vector", "scalar_tensor_tensor", out=yv, in0=ps,
                 scalar=sdr, in1=xv, op0=ALU.mult, op1=ALU.add,
                 reads=[ps, sdr, xv], writes=[yv])
            p.dma("qG", chv(g[dst], tt)[:, och, :], yv)


def emit_s7(p: P, src, dst):
    g = p.g
    ntt = p.ntt
    nck = p.tpc // 128
    SA, SH, ST, SM, CF = g["SA"], g["SH"], g["ST"], g["SM"], g["CF"]
    GRPS = [[0, 1, 2, 3], [4, 5, 6, 7]]
    IDY = CF[:, 132:260]
    PI = 3.14159265358979323846

    for nm, c0, w in (("l1b_lng", 784, 8), ("l1b_lnb", 792, 8),
                      ("l1b_pD", 800, 8)):
        p.dma("qS", SM[:, c0:c0 + w],
              g[nm].rearrange("(c p) -> p c", p=128),
              allow_slow_non_contiguous=True)
    p.dma("qS", SM[:, 808:817], g["s7sel"])
    emit_ln(p, g[src], [(g["s7_u"], F32)], gamma_c0=784, beta_c0=792)

    LR = SA[:, 5120:5124]; LI = SA[:, 5124:5128]
    p.dma("qS", LR, g["l1b_Lr"].rearrange("(c p) -> p c", p=128),
          allow_slow_non_contiguous=True)
    p.dma("qS", LI, g["l1b_Li"].rearrange("(c p) -> p c", p=128),
          allow_slow_non_contiguous=True)
    e1 = SA[:, 5128:5132]
    p.op("scalar", "activation", out=e1, in_=LR, func=ACT.Exp,
         reads=[LR], writes=[e1])
    ang = SA[:, 5132:5136]
    p.op("scalar", "activation", out=ang, in_=LI, func=ACT.Exp,
         reads=[LI], writes=[ang])
    modl = SA[:, 5136:5140]
    p.op("vector", "tensor_scalar", out=modl, in0=e1, scalar1=-1.0,
         scalar2=None, op0=ALU.mult, reads=[e1], writes=[modl])
    p.op("scalar", "activation", out=modl, in_=modl, func=ACT.Exp,
         reads=[modl], writes=[modl])
    y1 = SA[:, 5140:5144]; y2 = SA[:, 5144:5148]

    def rr_sin(out, phase):
        p.op("vector", "tensor_scalar", out=out, in0=ang,
             scalar1=1.0 / (2 * PI), scalar2=phase / (2 * PI) + MAGIC,
             op0=ALU.mult, op1=ALU.add, reads=[ang], writes=[out])
        p.op("vector", "tensor_scalar", out=out, in0=out, scalar1=MAGIC,
             scalar2=None, op0=ALU.subtract, reads=[out], writes=[out])
        p.op("vector", "scalar_tensor_tensor", out=out, in0=out,
             scalar=-2 * PI, in1=ang, op0=ALU.mult, op1=ALU.add,
             reads=[out, ang], writes=[out])
        if phase != 0.0:
            p.op("vector", "tensor_scalar", out=out, in0=out, scalar1=phase,
                 scalar2=None, op0=ALU.add, reads=[out], writes=[out])
        p.op("scalar", "activation", out=out, in_=out, func=ACT.Sin,
             reads=[out], writes=[out])

    rr_sin(y1, 0.0)
    rr_sin(y2, 0.5 * PI)
    lr = SA[:, 5148:5152]; li = SA[:, 5152:5156]
    p.op("vector", "tensor_tensor", out=lr, in0=modl, in1=y2, op=ALU.mult,
         reads=[modl, y2], writes=[lr])
    p.op("vector", "tensor_tensor", out=li, in0=modl, in1=y1, op=ALU.mult,
         reads=[modl, y1], writes=[li])
    d2 = SA[:, 5156:5160]
    p.op("vector", "tensor_tensor", out=d2, in0=e1, in1=e1, op=ALU.mult,
         reads=[e1], writes=[d2])
    t0 = SA[:, 5160:5164]
    p.op("vector", "tensor_tensor", out=t0, in0=ang, in1=ang, op=ALU.mult,
         reads=[ang], writes=[t0])
    p.op("vector", "tensor_add", out=d2, in0=d2, in1=t0, reads=[d2, t0],
         writes=[d2])
    p.op("vector", "reciprocal", out=d2, in_=d2, reads=[d2], writes=[d2])
    nr = SA[:, 5164:5168]
    p.op("vector", "tensor_scalar", out=nr, in0=lr, scalar1=-1.0,
         scalar2=None, op0=ALU.add, reads=[lr], writes=[nr])
    gr = SA[:, 5168:5172]; gi = SA[:, 5172:5176]
    p.op("vector", "tensor_tensor", out=gr, in0=nr, in1=e1, op=ALU.mult,
         reads=[nr, e1], writes=[gr])
    p.op("vector", "tensor_scalar", out=gr, in0=gr, scalar1=-1.0,
         scalar2=None, op0=ALU.mult, reads=[gr], writes=[gr])
    p.op("vector", "tensor_tensor", out=t0, in0=li, in1=ang, op=ALU.mult,
         reads=[li, ang], writes=[t0])
    p.op("vector", "tensor_add", out=gr, in0=gr, in1=t0, reads=[gr, t0],
         writes=[gr])
    p.op("vector", "tensor_tensor", out=gr, in0=gr, in1=d2, op=ALU.mult,
         reads=[gr, d2], writes=[gr])
    p.op("vector", "tensor_tensor", out=gi, in0=li, in1=e1, op=ALU.mult,
         reads=[li, e1], writes=[gi])
    p.op("vector", "tensor_scalar", out=gi, in0=gi, scalar1=-1.0,
         scalar2=None, op0=ALU.mult, reads=[gi], writes=[gi])
    p.op("vector", "tensor_tensor", out=t0, in0=nr, in1=ang, op=ALU.mult,
         reads=[nr, ang], writes=[t0])
    p.op("vector", "tensor_sub", out=gi, in0=gi, in1=t0, reads=[gi, t0],
         writes=[gi])
    p.op("vector", "tensor_tensor", out=gi, in0=gi, in1=d2, op=ALU.mult,
         reads=[gi, d2], writes=[gi])

    CrT = SA[:, 11264:15360].rearrange("p (c t) -> p c t", c=4)
    CiT = SA[:, 15360:19456].rearrange("p (c t) -> p c t", c=4)
    for nc4 in range(4):
        ar = SA[:, 0:1024]
        ai = SA[:, 1024:2048]
        p.dma("qS", ar, g["l1b_CrT"][nc4 * 128:(nc4 + 1) * 128, :])
        p.dma("qS", ai, g["l1b_CiT"][nc4 * 128:(nc4 + 1) * 128, :])
        grc = gr[:, nc4:nc4 + 1]
        gic = gi[:, nc4:nc4 + 1]
        p.op("vector", "tensor_scalar", out=CrT[:, nc4, :], in0=ar,
             scalar1=grc, scalar2=None, op0=ALU.mult,
             reads=[ar, grc], writes=[CrT[:, nc4, :]])
        t1 = SA[:, 2048:3072]
        p.op("vector", "tensor_scalar", out=t1, in0=ai, scalar1=gic,
             scalar2=None, op0=ALU.mult, reads=[ai, gic], writes=[t1])
        p.op("vector", "tensor_sub", out=CrT[:, nc4, :], in0=CrT[:, nc4, :],
             in1=t1, reads=[CrT[:, nc4, :], t1], writes=[CrT[:, nc4, :]])
        p.op("vector", "tensor_scalar", out=CiT[:, nc4, :], in0=ar,
             scalar1=gic, scalar2=None, op0=ALU.mult,
             reads=[ar, gic], writes=[CiT[:, nc4, :]])
        p.op("vector", "tensor_scalar", out=t1, in0=ai, scalar1=grc,
             scalar2=None, op0=ALU.mult, reads=[ai, grc], writes=[t1])
        p.op("vector", "tensor_add", out=CiT[:, nc4, :], in0=CiT[:, nc4, :],
             in1=t1, reads=[CiT[:, nc4, :], t1], writes=[CiT[:, nc4, :]])

    Vr = SA[:, 5184:5696].rearrange("p (c t) -> p c t", c=4)
    Vi = SA[:, 5696:6208].rearrange("p (c t) -> p c t", c=4)
    p.op("vector", "memset", SA[:, 5184:5696:128], 1.0,
         writes=[SA[:, 5184:5696:128]])
    p.op("vector", "memset", SA[:, 5696:6208:128], 0.0,
         writes=[SA[:, 5696:6208:128]])
    p.op("vector", "tensor_copy", out=SA[:, 5185:5696:128], in_=lr,
         reads=[lr], writes=[SA[:, 5185:5696:128]])
    p.op("vector", "tensor_copy", out=SA[:, 5697:6208:128], in_=li,
         reads=[li], writes=[SA[:, 5697:6208:128]])
    amr = SA[:, 5176:5180]; ami = SA[:, 5180:5184]

    def vcol(j):
        return SA[:, 5184 + j:5696:128]

    def vicol(j):
        return SA[:, 5696 + j:6208:128]

    mlen = 2
    while mlen < 128:
        p.op("vector", "tensor_tensor", out=amr, in0=vcol(mlen - 1),
             in1=lr, op=ALU.mult, reads=[vcol(mlen - 1), lr], writes=[amr])
        p.op("vector", "tensor_tensor", out=t0, in0=vicol(mlen - 1),
             in1=li, op=ALU.mult, reads=[vicol(mlen - 1), li], writes=[t0])
        p.op("vector", "tensor_sub", out=amr, in0=amr, in1=t0,
             reads=[amr, t0], writes=[amr])
        p.op("vector", "tensor_tensor", out=ami, in0=vcol(mlen - 1),
             in1=li, op=ALU.mult, reads=[vcol(mlen - 1), li], writes=[ami])
        p.op("vector", "tensor_tensor", out=t0, in0=vicol(mlen - 1),
             in1=lr, op=ALU.mult, reads=[vicol(mlen - 1), lr], writes=[t0])
        p.op("vector", "tensor_add", out=ami, in0=ami, in1=t0,
             reads=[ami, t0], writes=[ami])
        w = mlen
        arb = amr.unsqueeze(2).broadcast_to((128, 4, w))
        aib = ami.unsqueeze(2).broadcast_to((128, 4, w))
        tb1 = SA[:, 9792:9792 + 512].rearrange("p (c t) -> p c t", c=4)
        tb2 = SA[:, 10304:10304 + 512].rearrange("p (c t) -> p c t", c=4)
        p.op("vector", "tensor_tensor", out=tb1[:, :, 0:w], in0=Vr[:, :, 0:w],
             in1=arb, op=ALU.mult, reads=[Vr[:, :, 0:w], amr],
             writes=[tb1[:, :, 0:w]])
        p.op("vector", "tensor_tensor", out=tb2[:, :, 0:w], in0=Vi[:, :, 0:w],
             in1=aib, op=ALU.mult, reads=[Vi[:, :, 0:w], ami],
             writes=[tb2[:, :, 0:w]])
        p.op("vector", "tensor_sub", out=Vr[:, :, w:2 * w], in0=tb1[:, :, 0:w],
             in1=tb2[:, :, 0:w], reads=[tb1[:, :, 0:w], tb2[:, :, 0:w]],
             writes=[Vr[:, :, w:2 * w]])
        p.op("vector", "tensor_tensor", out=tb1[:, :, 0:w], in0=Vr[:, :, 0:w],
             in1=aib, op=ALU.mult, reads=[Vr[:, :, 0:w], ami],
             writes=[tb1[:, :, 0:w]])
        p.op("vector", "tensor_tensor", out=tb2[:, :, 0:w], in0=Vi[:, :, 0:w],
             in1=arb, op=ALU.mult, reads=[Vi[:, :, 0:w], amr],
             writes=[tb2[:, :, 0:w]])
        p.op("vector", "tensor_add", out=Vi[:, :, w:2 * w], in0=tb1[:, :, 0:w],
             in1=tb2[:, :, 0:w], reads=[tb1[:, :, 0:w], tb2[:, :, 0:w]],
             writes=[Vi[:, :, w:2 * w]])
        mlen *= 2
    a128r = SA[:, 5140:5144]
    a128i = SA[:, 5144:5148]
    p.op("vector", "tensor_tensor", out=a128r, in0=vcol(127), in1=lr,
         op=ALU.mult, reads=[vcol(127), lr], writes=[a128r])
    p.op("vector", "tensor_tensor", out=t0, in0=vicol(127), in1=li,
         op=ALU.mult, reads=[vicol(127), li], writes=[t0])
    p.op("vector", "tensor_sub", out=a128r, in0=a128r, in1=t0,
         reads=[a128r, t0], writes=[a128r])
    p.op("vector", "tensor_tensor", out=a128i, in0=vcol(127), in1=li,
         op=ALU.mult, reads=[vcol(127), li], writes=[a128i])
    p.op("vector", "tensor_tensor", out=t0, in0=vicol(127), in1=lr,
         op=ALU.mult, reads=[vicol(127), lr], writes=[t0])
    p.op("vector", "tensor_add", out=a128i, in0=a128i, in1=t0,
         reads=[a128i, t0], writes=[a128i])
    Vbr = SA[:, 6208:6720].rearrange("p (c t) -> p c t", c=4)
    Vbi = SA[:, 6720:7232].rearrange("p (c t) -> p c t", c=4)
    tb1 = SA[:, 9792:10304].rearrange("p (c t) -> p c t", c=4)
    tb2 = SA[:, 10304:10816].rearrange("p (c t) -> p c t", c=4)
    p.op("vector", "tensor_copy", out=tb1[:, :, :], in_=Vr[:, :, ::-1],
         reads=[Vr[:, :, :]], writes=[tb1[:, :, :]])
    p.op("vector", "tensor_copy", out=tb2[:, :, :], in_=Vi[:, :, ::-1],
         reads=[Vi[:, :, :]], writes=[tb2[:, :, :]])
    for c4 in range(4):
        for sb_, db_ in ((tb1, Vbr), (tb2, Vbi)):
            p.op("tensor", "transpose", out=g["ps4"][:, 0:128],
                 in_=sb_[:, c4, :], identity=IDY,
                 reads=[sb_[:, c4, :], IDY], writes=[g["ps4"][:, 0:128]])
            p.op("vector", "tensor_copy", out=db_[:, c4, :],
                 in_=g["ps4"][:, 0:128], reads=[g["ps4"][:, 0:128]],
                 writes=[db_[:, c4, :]])
    DAr = SA[:, 7232:7744].rearrange("p (c t) -> p c t", c=4)
    DAin = SA[:, 7744:8256].rearrange("p (c t) -> p c t", c=4)
    DAi = SA[:, 8256:8768].rearrange("p (c t) -> p c t", c=4)
    for c4 in range(4):
        p.op("vector", "tensor_scalar", out=DAr[:, c4, :], in0=IDY,
             scalar1=a128r[:, c4:c4 + 1], scalar2=None, op0=ALU.mult,
             reads=[IDY, a128r[:, c4:c4 + 1]], writes=[DAr[:, c4, :]])
        p.op("vector", "tensor_scalar", out=DAi[:, c4, :], in0=IDY,
             scalar1=a128i[:, c4:c4 + 1], scalar2=None, op0=ALU.mult,
             reads=[IDY, a128i[:, c4:c4 + 1]], writes=[DAi[:, c4, :]])
        p.op("vector", "tensor_scalar", out=DAin[:, c4, :], in0=DAi[:, c4, :],
             scalar1=-1.0, scalar2=None, op0=ALU.mult,
             reads=[DAi[:, c4, :]], writes=[DAin[:, c4, :]])

    sr = SA[:, 19456:23552].rearrange("p (c t) -> p c t", c=4)
    si = SA[:, 23552:27648].rearrange("p (c t) -> p c t", c=4)

    def zero_state():
        p.op("vector", "memset", sr[:, :, :], 0.0, writes=[sr[:, :, :]])
        p.op("vector", "memset", si[:, :, :], 0.0, writes=[si[:, :, :]])

    def chunk_z_update(ck):
        ut = SA[:, 0:1024].rearrange("p (c t) -> p c t", c=8)
        p.dma("qS", ut, chv(g["s7_u"], ck, 128))
        utm = SA[:, 1024:2048]
        for ch in range(8):
            p.op("tensor", "transpose", out=g["ps4"][:, 128:256],
                 in_=ut[:, ch, :], identity=IDY,
                 reads=[ut[:, ch, :], IDY], writes=[g["ps4"][:, 128:256]])
            p.op("vector", "tensor_copy",
                 out=utm[:, ch * 128:(ch + 1) * 128],
                 in_=g["ps4"][:, 128:256], reads=[g["ps4"][:, 128:256]],
                 writes=[utm[:, ch * 128:(ch + 1) * 128]])
        for nc4 in range(4):
            for half in range(2):
                uh = utm[:, half * 512:(half + 1) * 512]
                p.mm(g["ps2"][:, 0:512],
                     [(Vbr[:, nc4, :], uh),
                      (DAr[:, nc4, :], sr[:, nc4, half * 512:(half + 1) * 512]),
                      (DAin[:, nc4, :], si[:, nc4, half * 512:(half + 1) * 512])])
                p.mm(g["ps3"][:, 0:512],
                     [(Vbi[:, nc4, :], uh),
                      (DAr[:, nc4, :], si[:, nc4, half * 512:(half + 1) * 512]),
                      (DAi[:, nc4, :], sr[:, nc4, half * 512:(half + 1) * 512])])
                p.op("scalar", "copy",
                     out=sr[:, nc4, half * 512:(half + 1) * 512],
                     in_=g["ps2"][:, 0:512], reads=[g["ps2"][:, 0:512]],
                     writes=[sr[:, nc4, half * 512:(half + 1) * 512]])
                p.op("scalar", "copy",
                     out=si[:, nc4, half * 512:(half + 1) * 512],
                     in_=g["ps3"][:, 0:512], reads=[g["ps3"][:, 0:512]],
                     writes=[si[:, nc4, half * 512:(half + 1) * 512]])

    zero_state()
    for ck in range(nck):
        chunk_z_update(ck)
    p.dma("qG", g["s7_zin"][0].rearrange("(c p) t -> p c t", p=128), sr)
    p.dma("qG", g["s7_zin"][1].rearrange("(c p) t -> p c t", p=128), si)
    p.coll("AllGather", ALU.bypass, GRPS, [g["s7_zin"]], [g["s7_zout"]])
    import math as _m
    nsq = int(_m.log2(nck))
    pwr = SA[:, 5160:5164]; pwi = SA[:, 5164:5168]
    p.op("vector", "tensor_copy", out=pwr, in_=a128r, reads=[a128r],
         writes=[pwr])
    p.op("vector", "tensor_copy", out=pwi, in_=a128i, reads=[a128i],
         writes=[pwi])
    for _ in range(nsq):
        p.op("vector", "tensor_tensor", out=t0, in0=pwr, in1=pwr,
             op=ALU.mult, reads=[pwr], writes=[t0])
        p.op("vector", "tensor_tensor", out=amr, in0=pwi, in1=pwi,
             op=ALU.mult, reads=[pwi], writes=[amr])
        p.op("vector", "tensor_tensor", out=ami, in0=pwr, in1=pwi,
             op=ALU.mult, reads=[pwr, pwi], writes=[ami])
        p.op("vector", "tensor_sub", out=pwr, in0=t0, in1=amr,
             reads=[t0, amr], writes=[pwr])
        p.op("vector", "tensor_scalar", out=pwi, in0=ami, scalar1=2.0,
             scalar2=None, op0=ALU.mult, reads=[ami], writes=[pwi])
    pw2r = SA[:, 5168:5172]; pw2i = SA[:, 5172:5176]
    p.op("vector", "tensor_tensor", out=t0, in0=pwr, in1=pwr, op=ALU.mult,
         reads=[pwr], writes=[t0])
    p.op("vector", "tensor_tensor", out=amr, in0=pwi, in1=pwi, op=ALU.mult,
         reads=[pwi], writes=[amr])
    p.op("vector", "tensor_sub", out=pw2r, in0=t0, in1=amr, reads=[t0, amr],
         writes=[pw2r])
    p.op("vector", "tensor_tensor", out=ami, in0=pwr, in1=pwi, op=ALU.mult,
         reads=[pwr, pwi], writes=[ami])
    p.op("vector", "tensor_scalar", out=pw2i, in0=ami, scalar1=2.0,
         scalar2=None, op0=ALU.mult, reads=[ami], writes=[pw2i])
    zero_state()
    ones4 = SA[:, 5176:5180]
    p.op("vector", "memset", ones4, 1.0, writes=[ones4])
    for pp in range(3):
        wpr = SA[:, 5180:5184]
        wpi = SA[:, 5140:5144]
        p.op("vector", "memset", wpr, 0.0, writes=[wpr])
        p.op("vector", "memset", wpi, 0.0, writes=[wpi])
        for kk, (rr_, ii_) in enumerate(((ones4, None), (pwr, pwi),
                                         (pw2r, pw2i))):
            selc = SM[:, 808 + pp * 3 + kk:809 + pp * 3 + kk]
            p.op("vector", "scalar_tensor_tensor", out=wpr, in0=rr_,
                 scalar=selc, in1=wpr, op0=ALU.mult, op1=ALU.add,
                 reads=[rr_, selc, wpr], writes=[wpr])
            if ii_ is not None:
                p.op("vector", "scalar_tensor_tensor", out=wpi, in0=ii_,
                     scalar=selc, in1=wpi, op0=ALU.mult, op1=ALU.add,
                     reads=[ii_, selc, wpi], writes=[wpi])
        zr_ = SA[:, 0:4096].rearrange("p (c t) -> p c t", c=4)
        zi_ = SA[:, 4096 + 0:4096 + 4096].rearrange("p (c t) -> p c t", c=4) \
            if False else SA[:, 0:4096].rearrange("p (c t) -> p c t", c=4)
        for nc4 in range(4):
            p.dma("qS", zr_[:, nc4, :],
                  g["s7_zout"][pp, 0][nc4 * 128:(nc4 + 1) * 128, :])
        for nc4 in range(4):
            wr_ = wpr[:, nc4:nc4 + 1]
            wi_ = wpi[:, nc4:nc4 + 1]
            p.op("vector", "scalar_tensor_tensor", out=sr[:, nc4, :],
                 in0=zr_[:, nc4, :], scalar=wr_, in1=sr[:, nc4, :],
                 op0=ALU.mult, op1=ALU.add,
                 reads=[zr_[:, nc4, :], wr_, sr[:, nc4, :]],
                 writes=[sr[:, nc4, :]])
            p.op("vector", "scalar_tensor_tensor", out=si[:, nc4, :],
                 in0=zr_[:, nc4, :], scalar=wi_, in1=si[:, nc4, :],
                 op0=ALU.mult, op1=ALU.add,
                 reads=[zr_[:, nc4, :], wi_, si[:, nc4, :]],
                 writes=[si[:, nc4, :]])
        for nc4 in range(4):
            p.dma("qS", zr_[:, nc4, :],
                  g["s7_zout"][pp, 1][nc4 * 128:(nc4 + 1) * 128, :])
        for nc4 in range(4):
            wr_ = wpr[:, nc4:nc4 + 1]
            wi_ = wpi[:, nc4:nc4 + 1]
            nwi = SA[:, 5144:5148]
            p.op("vector", "tensor_scalar", out=nwi, in0=wpi, scalar1=-1.0,
                 scalar2=None, op0=ALU.mult, reads=[wpi], writes=[nwi])
            p.op("vector", "scalar_tensor_tensor", out=sr[:, nc4, :],
                 in0=zr_[:, nc4, :], scalar=nwi[:, nc4:nc4 + 1],
                 in1=sr[:, nc4, :], op0=ALU.mult, op1=ALU.add,
                 reads=[zr_[:, nc4, :], nwi[:, nc4:nc4 + 1], sr[:, nc4, :]],
                 writes=[sr[:, nc4, :]])
            p.op("vector", "scalar_tensor_tensor", out=si[:, nc4, :],
                 in0=zr_[:, nc4, :], scalar=wr_, in1=si[:, nc4, :],
                 op0=ALU.mult, op1=ALU.add,
                 reads=[zr_[:, nc4, :], wr_, si[:, nc4, :]],
                 writes=[si[:, nc4, :]])

    for ck in range(nck):
        yev = SA[:, 8768:9792].rearrange("p (c t) -> p c t", c=8)
        for nc4 in range(4):
            Pr = SA[:, 2048:3072]
            Pi = SA[:, 3072:4096]
            mr = SA[:, 4096:5120]
            lrc = lr[:, nc4:nc4 + 1]
            lic = li[:, nc4:nc4 + 1]
            p.op("vector", "tensor_scalar", out=mr, in0=sr[:, nc4, :],
                 scalar1=lrc, scalar2=None, op0=ALU.mult,
                 reads=[sr[:, nc4, :], lrc], writes=[mr])
            p.op("vector", "scalar_tensor_tensor", out=mr, in0=si[:, nc4, :],
                 scalar=lic, in1=mr, op0=ALU.mult, op1=ALU.subtract,
                 reads=[si[:, nc4, :], lic, mr], writes=[mr])
            p.op("vector", "tensor_scalar", out=mr, in0=mr, scalar1=-1.0,
                 scalar2=None, op0=ALU.mult, reads=[mr], writes=[mr])
            mi = SA[:, 1024:2048]
            p.op("vector", "tensor_scalar", out=mi, in0=si[:, nc4, :],
                 scalar1=lrc, scalar2=None, op0=ALU.mult,
                 reads=[si[:, nc4, :], lrc], writes=[mi])
            p.op("vector", "scalar_tensor_tensor", out=mi, in0=sr[:, nc4, :],
                 scalar=lic, in1=mi, op0=ALU.mult, op1=ALU.add,
                 reads=[sr[:, nc4, :], lic, mi], writes=[mi])
            p.op("vector", "tensor_tensor", out=Pr, in0=CrT[:, nc4, :],
                 in1=mr, op=ALU.mult, reads=[CrT[:, nc4, :], mr], writes=[Pr])
            t1b = SA[:, 0:1024]
            p.op("vector", "tensor_tensor", out=t1b, in0=CiT[:, nc4, :],
                 in1=mi, op=ALU.mult, reads=[CiT[:, nc4, :], mi],
                 writes=[t1b])
            p.op("vector", "tensor_sub", out=Pr, in0=Pr, in1=t1b,
                 reads=[Pr, t1b], writes=[Pr])
            p.op("vector", "tensor_tensor", out=Pi, in0=CrT[:, nc4, :],
                 in1=mi, op=ALU.mult, reads=[CrT[:, nc4, :], mi], writes=[Pi])
            p.op("vector", "scalar_tensor_tensor", out=Pi,
                 in0=CiT[:, nc4, :], scalar=1.0, in1=Pi,
                 op0=ALU.bypass, op1=ALU.add,
                 reads=[CiT[:, nc4, :], Pi], writes=[Pi]) if False else None
            p.op("vector", "tensor_tensor", out=t1b, in0=CiT[:, nc4, :],
                 in1=mr, op=ALU.mult, reads=[CiT[:, nc4, :], mr],
                 writes=[t1b])
            p.op("vector", "tensor_add", out=Pi, in0=Pi, in1=t1b,
                 reads=[Pi, t1b], writes=[Pi])
            p.op("vector", "tensor_scalar", out=Pi, in0=Pi, scalar1=-1.0,
                 scalar2=None, op0=ALU.mult, reads=[Pi], writes=[Pi])
            for dt8 in range(8):
                psl = g["ps%d" % dt8][:, 0:128]
                p.mm(psl, [(Pr[:, dt8 * 128:(dt8 + 1) * 128], Vr[:, nc4, :]),
                           (Pi[:, dt8 * 128:(dt8 + 1) * 128], Vi[:, nc4, :])],
                     cont=(nc4 > 0), stop=(nc4 == 3))
        for dt8 in range(8):
            psl = g["ps%d" % dt8][:, 0:128]
            p.op("vector", "tensor_copy", out=yev[:, dt8, :], in_=psl,
                 reads=[psl], writes=[yev[:, dt8, :]])
        p.dma("qG", chv(g["s7_yi"], ck, 128), yev)
        chunk_z_update(ck)

    Kcm = SA[:, 4096:5120].rearrange("p (c t) -> p c t", c=8)
    Vin = SA[:, 9792:10304].rearrange("p (c t) -> p c t", c=4)
    for c4 in range(4):
        p.op("vector", "tensor_scalar", out=Vin[:, c4, :], in0=Vi[:, c4, :],
             scalar1=-1.0, scalar2=None, op0=ALU.mult,
             reads=[Vi[:, c4, :]], writes=[Vin[:, c4, :]])
    for dt8 in range(8):
        psl = g["ps%d" % dt8][:, 0:128]
        p.mm(psl, [(CrT[:, nc4, dt8 * 128:(dt8 + 1) * 128], Vr[:, nc4, :])
                   for nc4 in range(4)]
             + [(CiT[:, nc4, dt8 * 128:(dt8 + 1) * 128], Vin[:, nc4, :])
                for nc4 in range(4)])
        p.op("vector", "tensor_copy", out=Kcm[:, dt8, :], in_=psl,
             reads=[psl], writes=[Kcm[:, dt8, :]])

    for ch in range(8):
        uv = SA[:, 11264:11264 + p.tpc]
        yv = SA[:, 11264 + p.tpc:11264 + 2 * p.tpc]
        xv = SA[:, 11264 + 2 * p.tpc:11264 + 3 * p.tpc]
        p.dma("qS", uv, g["s7_u"].rearrange("(c p) t -> p c t", p=128)[:, ch, :])
        p.dma("qS", yv, g["s7_yi"].rearrange("(c p) t -> p c t", p=128)[:, ch, :])
        p.dma("qS", xv, g[src].rearrange("(c p) t -> p c t", p=128)[:, ch, :])
        uw = uv.rearrange("p (n t) -> p n t", n=nck)
        yw = yv.rearrange("p (n t) -> p n t", n=nck)
        for l in range(128):
            kc_ = Kcm[:, ch, l:l + 1]
            p.op("vector", "scalar_tensor_tensor", out=yw[:, :, l:128],
                 in0=uw[:, :, 0:128 - l], scalar=kc_, in1=yw[:, :, l:128],
                 op0=ALU.mult, op1=ALU.add,
                 reads=[uw[:, :, 0:128 - l], kc_, yw[:, :, l:128]],
                 writes=[yw[:, :, l:128]])
        dcol = SM[:, 800 + ch:801 + ch]
        p.op("vector", "scalar_tensor_tensor", out=yv, in0=uv, scalar=dcol,
             in1=yv, op0=ALU.mult, op1=ALU.add, reads=[uv, dcol, yv],
             writes=[yv])
        p.op("vector", "tensor_add", out=yv, in0=yv, in1=xv,
             reads=[yv, xv], writes=[yv])
        p.dma("qG", g[dst].rearrange("(c p) t -> p c t", p=128)[:, ch, :], yv)



def build_program(tpc=TPC, n_stages=None):
    if n_stages is None:
        n_stages = N_STAGES
    p = P(tpc=tpc)
    g = p.g
    p.din("x0", [D, tpc])
    p.din("identf", [128, 128], F32)
    p.din("cmask", [128, 4])
    p.din("cmaskc", [128, 4])
    p.din("l0a_wmix", [D]); p.din("l0a_wffn", [D])
    p.din("l0a_Wg", [D, D]); p.din("l0a_Wv", [D, D]); p.din("l0a_Wd", [D, D])
    p.din("l0a_bg", [D]); p.din("l0a_bv", [D]); p.din("l0a_bd", [D])
    p.din("l0a_Wgate", [D, 3072]); p.din("l0a_Wup", [D, 3072])
    p.din("l0a_Wout", [3072, D])
    if n_stages >= 2:
        p.din("moe_gwT", [D, 16])
        p.din("moe_gb", [1, 16])
        p.din("moe_w1T", [16 * 1024, 768])
        p.din("moe_w3T", [16 * 1024, 768])
        p.din("moe_w2T", [16 * 768, 1024])
        p.din("moe_ws1T", [1024, 768])
        p.din("moe_ws3T", [1024, 768])
        p.din("moe_ws2T", [768, 1024])
    p.dout("y", [D, tpc])
    for nm in ("xs0", "xs1", "ln0f", "s1_xscan", "s1_a", "s1_h", "s1_P",
               "s1_xpi"):
        p.dtmp(nm, [D, tpc])
    if n_stages >= 2:
        for nm in ("xs2", "xs3", "xs4", "moe_lnf"):
            p.dtmp(nm, [D, tpc])
        p.dtmp("moe_lnh", [D, tpc], BF16)
        p.dtmp("moe_comb", [16, tpc])
        p.dtmp("moe_hact", [17 * 768, tpc], BF16)
        for nm, shp in (("h_w1T", [16 * 1024, 768]),
                        ("h_w3T", [16 * 1024, 768]),
                        ("h_w2T", [16 * 768, 1024]),
                        ("h_ws1T", [1024, 768]), ("h_ws3T", [1024, 768]),
                        ("h_ws2T", [768, 1024])):
            p.dtmp(nm, shp, BF16)
    if n_stages >= 3:
        ntt = tpc // TT
        ntt4 = ntt // 4
        n_theta = 8 * ntt * (ntt + 1)
        p.din("l0b_W", [D, 3072])
        p.din("l0b_bq", [D])
        p.din("l0b_bk", [D])
        p.din("l0b_bv", [1, D])
        p.din("s3_theta", [128, n_theta])
        p.din("s3_cr", [128, 512])
        p.dtmp("h_l0bW", [D, 3072], BF16)
        p.dtmp("s3_lnh", [D, tpc], BF16)
        p.dtmp("s3_ynall", [4, D, tpc], BF16)
        p.din("qsel", [128, 4])
        p.dtmp("s3_k", [D, 4 * tpc], BF16)
        p.dtmp("s3_v", [4 * tpc, D], BF16)
        p.dtmp("s3_q", [D, tpc], BF16)
        p.dtmp("s3_oin", [ntt, D, TT], BF16)
        p.dtmp("s3_oout", [4, ntt, D, TT], BF16)
    if n_stages >= 5:
        p.din("l1a_Wq", [D, D]); p.din("l1a_Wk", [D, D])
        p.din("l1a_Wv", [D, D]); p.din("l1a_Wo", [D, D])
        p.din("l1a_Wb", [D, 4]); p.din("l1a_idW", [D, 4])
        p.din("l1a_qcw", [128, 32]); p.din("l1a_kcw", [128, 32])
        p.din("l1a_vcw", [128, 32])
        p.din("l1a_f3", [128, 24]); p.din("l1a_f7", [128, 56])
        p.din("l1a_f15", [128, 120]); p.din("l1a_f31", [128, 248])
        p.din("l1a_rW1", [6144, 2048]); p.din("l1a_rb1", [2048])
        p.din("l1a_rW2", [2048, 128]); p.din("l1a_rb2", [128, 1])
        p.din("l1a_idb", [4, 1]); p.din("l1a_ltau", [1, 4])
        p.din("l1a_onw", [128, 2])
        p.din("chsel", [128, 4]); p.din("prevsel", [128, 4])
        for nm, shp, dt in (
                ("h_Wq", [D, D], BF16), ("h_Wk", [D, D], BF16),
                ("h_Wv1", [D, D], BF16), ("h_Wo", [D, D], BF16),
                ("h_rW1", [6144, 2048], BF16), ("h_rW2", [2048, 128], BF16),
                ("s5_lnh", [D, tpc], BF16), ("s5_hin", [D, 48], BF16),
                ("s5_hout", [4, D, 48], BF16), ("s5_xE", [D, tpc + 48], BF16),
                ("s5_qp", [D, tpc + 48], F32), ("s5_kp", [D, tpc + 48], F32),
                ("s5_vp", [D, tpc + 48], F32),
                ("s5_qd", [D, tpc], F32), ("s5_kd", [D, tpc], F32),
                ("s5_vd", [D, tpc + 32], F32), ("s5_beta", [4, tpc], F32),
                ("s5_qn", [D, tpc], F32), ("s5_kn", [D, tpc], F32),
                ("s5_kb", [D, tpc], F32), ("s5_vb", [D, tpc], F32),
                ("s5_d0", [D, tpc], F32), ("s5_GT", [D, tpc], F32),
                ("s5_sfin", [4, 256, 512], F32),
                ("s5_sfout", [4, 4, 256, 512], F32),
                ("s5_delta", [D, tpc], BF16),
                ("s5_f3", [D, tpc], BF16), ("s5_f7", [D, tpc], BF16),
                ("s5_f15", [D, tpc], BF16), ("s5_f31", [D, tpc], BF16),
                ("s5_hmid", [2048, tpc], BF16),
                ("s5_probs", [20, tpc], F32), ("s5_gate", [8, tpc], F32),
                ("s5_on", [D, tpc], BF16), ("xs5", [D, tpc], F32)):
            p.dtmp(nm, shp, dt)
    if n_stages >= 6:
        p.din("l1f_wg", [2816, 1024]); p.din("l1f_wu", [2816, 1024])
        p.din("l1f_wd", [1024, 2816])
        p.din("l1f_wgT", [1024, 2816]); p.din("l1f_wuT", [1024, 2816])
        p.din("l1f_wdT", [2816, 1024])
        p.din("l1f_clx", [1, 1]); p.din("l1f_cld", [1, 1])
        for nm, shp, dt in (
                ("s6_lnf", [D, tpc], F32),
                ("s6_sg", [2816, 1], F32), ("s6_su", [2816, 1], F32),
                ("s6_sd", [1024, 1], F32),
                ("s6_sgr", [1, 2816], F32), ("s6_sur", [1, 2816], F32),
                ("s6_sdr", [1, 1024], F32),
                ("h_wgT", [1024, 2816], BF16), ("h_wuT", [1024, 2816], BF16),
                ("h_wdT", [2816, 1024], BF16),
                ("s6_gin", [1, 1], F32), ("s6_gout", [1, 1], F32),
                ("s6_gout2", [1, 1], F32),
                ("s6_xq", [D, tpc], BF16), ("s6_h", [2816, tpc], F32),
                ("s6_hq", [2816, tpc], BF16),
                ("xs6", [D, tpc], F32), ("xs7", [D, tpc], F32),
                ("xs8", [D, tpc], F32)):
            p.dtmp(nm, shp, dt)
    if n_stages >= 7:
        p.din("l1b_Lr", [512]); p.din("l1b_Li", [512])
        p.din("l1b_CrT", [512, 1024]); p.din("l1b_CiT", [512, 1024])
        p.din("l1b_pD", [D]); p.din("l1b_lng", [D]); p.din("l1b_lnb", [D])
        p.din("s7sel", [128, 9])
        p.dtmp("s7_u", [D, tpc])
        p.dtmp("s7_yi", [D, tpc])
        p.dtmp("s7_zin", [2, 512, 1024])
        p.dtmp("s7_zout", [4, 2, 512, 1024])
    p.dtmp("s1_yn", [D, tpc], BF16)
    p.dtmp("s1_hact", [3072, tpc], BF16)
    p.dtmp("s1_rms", [tpc // TT, TT])
    for nm, shp in (("h_Wg", [D, D]), ("h_Wv", [D, D]), ("h_Wd", [D, D]),
                    ("h_Wgate", [D, 3072]), ("h_Wup", [D, 3072]),
                    ("h_Wout", [3072, D])):
        p.dtmp(nm, shp, BF16)
    p.dtmp("cc_s1_in", [2, D])
    p.dtmp("cc_s1_out", [4, 2, D])
    p.sbuf("SA", [128, 27648], F32)
    p.sbuf("SH", [128, 32768], BF16)
    p.sbuf("SM", [128, 832], F32)
    p.sbuf("ST", [1, 4096], F32)
    p.sbuf("CF", [128, 384], F32)
    p.sbuf("CH", [128, 132], BF16)
    p.sbuf("SHB", [128, 1024], BF16)
    for i in range(8):
        p.psum("ps%d" % i, [128, 512], F32)

    p.dma("qG", g["xs0"], g["x0"])
    emit_prep(p)
    emit_cast_weights(p, n_stages)
    stream = "xs0"
    if n_stages >= 1:
        emit_ln(p, g["xs0"], [(g["ln0f"], F32)], rms_eps=1e-6,
                rms_dst=g["s1_rms"])
        emit_s1(p)
        stream = "xs1"
    if n_stages >= 2:
        emit_s2(p, "xs1", "xs2")
        stream = "xs2"
    if n_stages >= 3:
        emit_s3(p)
        stream = "xs3"
    if n_stages >= 4:
        emit_s2(p, "xs3", "xs4")
        stream = "xs4"
    if n_stages >= 5:
        emit_s5(p, "xs4", "xs5")
        stream = "xs5"
    if n_stages >= 6:
        emit_s6(p, "xs5", "xs6", do_quant=True, call_id=0)
        stream = "xs6"
    if n_stages >= 7:
        emit_s7(p, "xs6", "xs7")
        stream = "xs7"
    if n_stages >= 8:
        emit_s6(p, "xs7", "xs8", do_quant=False, call_id=1)
        stream = "xs8"
    t = p.dma("qG", g["y"], g[stream])
    p._wait("gpsimd", t)
    p.finish()
    return p


def _pp(w):
    C, K = w.shape
    return np.ascontiguousarray(
        w.reshape(C // 128, 128, K).transpose(1, 0, 2).reshape(128, -1))


def make_host_inputs(inputs, core_id, tpc=TPC):
    b, qs = core_id // 4, core_id % 4
    x = np.asarray(inputs["L0_attn_a_x"])
    sl = slice(qs * tpc, (qs + 1) * tpc)
    cm = np.zeros((128, 4), np.float32)
    cm[:, :qs] = 1.0
    m = {
        "x0": np.ascontiguousarray(x[b, sl, :].T),
        "identf": np.eye(128, dtype=np.float32),
        "cmask": cm,
        "cmaskc": 1.0 - cm,
        "l0a_wmix": np.asarray(inputs["L0_attn_a_w_rms_mix"]),
        "l0a_wffn": np.asarray(inputs["L0_attn_a_w_rms_ffn"]),
        "l0a_Wg": np.asarray(inputs["L0_attn_a_Wg"]),
        "l0a_Wv": np.asarray(inputs["L0_attn_a_Wv"]),
        "l0a_Wd": np.asarray(inputs["L0_attn_a_Wd"]),
        "l0a_bg": np.asarray(inputs["L0_attn_a_bg"]),
        "l0a_bv": np.asarray(inputs["L0_attn_a_bv"]),
        "l0a_bd": np.asarray(inputs["L0_attn_a_bd"]),
        "l0a_Wgate": np.asarray(inputs["L0_attn_a_W_gate"]),
        "l0a_Wup": np.asarray(inputs["L0_attn_a_W_up"]),
        "l0a_Wout": np.asarray(inputs["L0_attn_a_W_out"]),
    }
    if N_STAGES >= 2:
        w1 = np.asarray(inputs["L0_ffn_w1"])
        w2 = np.asarray(inputs["L0_ffn_w2"])
        w3 = np.asarray(inputs["L0_ffn_w3"])
        m.update({
            "moe_gwT": np.ascontiguousarray(np.asarray(inputs["L0_ffn_gate_w"]).T),
            "moe_gb": np.asarray(inputs["L0_ffn_gate_bias"])[None, :],
            "moe_w1T": np.ascontiguousarray(w1.transpose(0, 2, 1)).reshape(16 * 1024, 768),
            "moe_w3T": np.ascontiguousarray(w3.transpose(0, 2, 1)).reshape(16 * 1024, 768),
            "moe_w2T": np.ascontiguousarray(w2.transpose(0, 2, 1)).reshape(16 * 768, 1024),
            "moe_ws1T": np.ascontiguousarray(np.asarray(inputs["L0_ffn_ws1"]).T),
            "moe_ws3T": np.ascontiguousarray(np.asarray(inputs["L0_ffn_ws3"]).T),
            "moe_ws2T": np.ascontiguousarray(np.asarray(inputs["L0_ffn_ws2"]).T),
        })
    if N_STAGES >= 3:
        ntt = tpc // TT
        Wb = np.asarray(inputs["L0_attn_b_W"])
        bb = np.asarray(inputs["L0_attn_b_b"])
        n_theta = 8 * ntt * (ntt + 1)
        th = np.empty(n_theta, np.float32)
        i = 0
        for mm_ in range(ntt):
            s_glob = qs + 4 * mm_
            for kt in range(16 * mm_ + 16):
                true_tiles = 4 * (s_glob + 1) * (TT // 512) if False else \
                    (TT * (s_glob + 1)) // 128
                if kt < true_tiles:
                    th[i] = 128.0 * kt - float(TT * s_glob)
                else:
                    th[i] = 1.0e9
                i += 1
        cr = (np.arange(512, dtype=np.float32)[None, :]
              - np.arange(128, dtype=np.float32)[:, None])
        qselm = np.zeros((128, 4), np.float32)
        qselm[:, qs] = 1.0
        m.update({
            "qsel": qselm,
            "l0b_W": Wb,
            "l0b_bq": bb[0:1024],
            "l0b_bk": bb[1024:2048],
            "l0b_bv": bb[None, 2048:3072],
            "s3_theta": np.ascontiguousarray(
                np.broadcast_to(th[None, :], (128, n_theta))),
            "s3_cr": np.ascontiguousarray(cr),
        })
    if N_STAGES >= 5:
        rW2 = np.zeros((2048, 128), np.float32)
        rW2[:, 0:20] = np.asarray(inputs["L1_attn_a_router_W2"])
        rb2 = np.zeros((128, 1), np.float32)
        rb2[0:20, 0] = np.asarray(inputs["L1_attn_a_router_b2"])
        chs = np.zeros((128, 4), np.float32)
        chs[:, qs] = 1.0
        prv = np.zeros((128, 4), np.float32)
        if qs >= 1:
            prv[:, qs - 1] = 1.0
        m.update({
            "l1a_Wq": np.asarray(inputs["L1_attn_a_Wq"]),
            "l1a_Wk": np.asarray(inputs["L1_attn_a_Wk"]),
            "l1a_Wv": np.asarray(inputs["L1_attn_a_Wv"]),
            "l1a_Wo": np.asarray(inputs["L1_attn_a_Wo"]),
            "l1a_Wb": np.asarray(inputs["L1_attn_a_Wb"]),
            "l1a_idW": np.asarray(inputs["L1_attn_a_id_gate_W"]),
            "l1a_qcw": _pp(np.asarray(inputs["L1_attn_a_q_conv_w"])),
            "l1a_kcw": _pp(np.asarray(inputs["L1_attn_a_k_conv_w"])),
            "l1a_vcw": _pp(np.asarray(inputs["L1_attn_a_v_conv_w"])),
            "l1a_f3": _pp(np.asarray(inputs["L1_attn_a_fir_w3"]).reshape(1024, 3)),
            "l1a_f7": _pp(np.asarray(inputs["L1_attn_a_fir_w7"]).reshape(1024, 7)),
            "l1a_f15": _pp(np.asarray(inputs["L1_attn_a_fir_w15"]).reshape(1024, 15)),
            "l1a_f31": _pp(np.asarray(inputs["L1_attn_a_fir_w31"]).reshape(1024, 31)),
            "l1a_rW1": np.asarray(inputs["L1_attn_a_router_W1"]),
            "l1a_rb1": np.asarray(inputs["L1_attn_a_router_b1"]),
            "l1a_rW2": rW2,
            "l1a_rb2": rb2,
            "l1a_idb": np.asarray(inputs["L1_attn_a_id_gate_b"])[:, None],
            "l1a_ltau": np.asarray(inputs["L1_attn_a_log_tau"])[None, :],
            "l1a_onw": _pp(np.asarray(inputs["L1_attn_a_o_norm_w"])[:, None]),
            "chsel": chs,
            "prevsel": prv,
        })
    if N_STAGES >= 6:
        wg_ = np.asarray(inputs["L1_ffn_w_gate"])
        wu_ = np.asarray(inputs["L1_ffn_w_up"])
        wd_ = np.asarray(inputs["L1_ffn_w_down"])
        m.update({
            "l1f_wg": wg_, "l1f_wu": wu_, "l1f_wd": wd_,
            "l1f_wgT": np.ascontiguousarray(wg_.T),
            "l1f_wuT": np.ascontiguousarray(wu_.T),
            "l1f_wdT": np.ascontiguousarray(wd_.T),
            "l1f_clx": np.asarray(inputs["L1_ffn_clip_x"])[:, None],
            "l1f_cld": np.asarray(inputs["L1_ffn_clip_down"])[:, None],
        })
    if N_STAGES >= 7:
        s7s = np.zeros((128, 9), np.float32)
        for pp_ in range(3):
            for kk_ in range(3):
                if qs == pp_ + kk_ + 1:
                    s7s[:, pp_ * 3 + kk_] = 1.0
        m.update({
            "l1b_Lr": np.asarray(inputs["L1_attn_b_Lambda_real"]),
            "l1b_Li": np.asarray(inputs["L1_attn_b_Lambda_imag"]),
            "l1b_CrT": np.ascontiguousarray(np.asarray(inputs["L1_attn_b_C_real"]).T),
            "l1b_CiT": np.ascontiguousarray(np.asarray(inputs["L1_attn_b_C_imag"]).T),
            "l1b_pD": np.asarray(inputs["L1_attn_b_param_D"]),
            "l1b_lng": np.asarray(inputs["L1_attn_b_ln_gamma"]),
            "l1b_lnb": np.asarray(inputs["L1_attn_b_ln_beta"]),
            "s7sel": s7s,
        })
    return m


def kernel(**inputs):
    p = build_program()
    in_maps = [make_host_inputs(inputs, c) for c in range(NCORE)]
    res = run_bass_kernel_spmd(p.nc, in_maps, core_ids=list(range(NCORE)))
    p.st.close()
    out = np.empty((B, S, D), dtype=np.float32)
    for c in range(NCORE):
        b, qs = c // 4, c % 4
        out[b, qs * TPC:(qs + 1) * TPC, :] = res.results[c]["y"].T
    return out
